# version 18
# baseline (speedup 1.0000x reference)
import contextlib
import numpy as np
import ml_dtypes
import concourse.bass as bass
import concourse.mybir as mybir
from concourse.bass_utils import run_bass_kernel_spmd

F32 = mybir.dt.float32
BF16 = mybir.dt.bfloat16
I32 = mybir.dt.int32
AF = mybir.ActivationFunctionType
ALU = mybir.AluOpType
AX = mybir.AxisListType

NFC = 32
TTA = 16
SCALE_ATT = float(1.0 / np.sqrt(128.0))
EPS = 1e-5
ER = 224
KB = 1024
DEBUG = False


def build_nc():
    nc = bass.Bass("TRN2", target_bir_lowering=False, debug=False)

    dram = lambda n, sh, ty, kind: nc.dram_tensor(n, list(sh), ty, kind=kind).ap()
    x_t = dram("x_t", (TTA, 128, NFC, 256), F32, "ExternalInput")
    xm_t = dram("xm_t", (128, NFC, 1024), F32, "ExternalInput")
    wq_t = dram("wq_t", (2, 128, NFC * 512), F32, "ExternalInput")
    wkv_t = dram("wkv_t", (128, NFC * 512), F32, "ExternalInput")
    wd_t = dram("wd_t", (NFC, 128, NFC * 128), F32, "ExternalInput")
    wb_t = dram("wb_t", (NFC, 128, NFC * 128), F32, "ExternalInput")
    wg_t = dram("wg_t", (128, NFC * 28), F32, "ExternalInput")
    wa_t = dram("wa_t", (2, 128, 16 * ER), F32, "ExternalInput")
    wbt0_t = dram("wbt0_t", (128, 4096), F32, "ExternalInput")
    wbt1_t = dram("wbt1_t", (96, 4096), F32, "ExternalInput")
    identf = dram("identf", (128, 128), F32, "ExternalInput")
    onescol = dram("onescol", (128, 1), BF16, "ExternalInput")
    onesrow = dram("onesrow", (1, 128), F32, "ExternalInput")
    maskdiag = dram("maskdiag", (128, 512), BF16, "ExternalInput")
    emat = dram("emat", (28, ER), BF16, "ExternalInput")
    idx_rb = dram("idx_rb", (128, 32), I32, "ExternalInput")
    outT = dram("outT", (4096, 1024), F32, "ExternalOutput")
    agin = [dram(f"agin{q}", (1024, 1024), BF16, "Internal") for q in range(4)]
    agout = dram("agout", (16384, 1024), BF16, "Internal")
    if DEBUG:
        dbg_k = dram("dbg_k", (2, 128, 4096), BF16, "ExternalOutput")
        dbg_v = dram("dbg_v", (2, 128, 4096), BF16, "ExternalOutput")
        dbg_q = dram("dbg_q", (2, 128, 4096), BF16, "ExternalOutput")
        dbg_aq = dram("dbg_aq", (128, 8192), BF16, "ExternalOutput")
        dbg_am = dram("dbg_am", (128, 32768), BF16, "ExternalOutput")
        dbg_x1 = dram("dbg_x1", (128, 32768), BF16, "ExternalOutput")
        dbg_x2 = dram("dbg_x2", (128, 32768), BF16, "ExternalOutput")
        dbg_gt = dram("dbg_gt", (28, 1024), BF16, "ExternalOutput")
        dbg_hg = dram("dbg_hg", (128, 1024), BF16, "ExternalOutput")
        dbg_lg = dram("dbg_lg", (128, 224), F32, "ExternalOutput")
        dbg_rc = dram("dbg_rc", (1, 512), F32, "ExternalOutput")
        dbg_ca = dram("dbg_ca", (2, 128, 512), F32, "ExternalOutput")
        dbg_bc = dram("dbg_bc", (2, 128, 512), F32, "ExternalOutput")

    ctx = contextlib.ExitStack()
    ctx.__enter__()

    ARENA = 187 * KB
    arena = nc.alloc_sbuf_tensor("arena", [128, ARENA], mybir.dt.uint8)
    base = nc.lookup_mloc(arena).addr

    def At(name, shape, ty, off):
        assert off + int(np.prod(shape[1:])) * mybir.dt.size(ty) <= ARENA, name
        return nc.alloc_sbuf_tensor_at(name, list(shape), ty, offset=base + off)

    cpool = At("cpool", [128, 16], F32, 0)
    ones_sb = At("ones_sb", [128, 32], BF16, 64)
    idx_sb = At("idx_sb", [128, 32], I32, 192)
    identf_sb = At("identf_sb", [128, 128], F32, 320)
    mask_sb = At("mask_sb", [128, 512], BF16, 832)
    onesrow_sb = At("onesrow_sb", [1, 128], F32, 1856)
    emat_sb = At("emat_sb", [28, ER], BF16, 2368)
    wg_sb = At("wg_sb", [128, NFC * 28], BF16, 2880)

    q_sb = [At(f"q_sb{p}", [128, 4 * 4096], BF16, 5 * KB + p * 32 * KB) for p in range(2)]
    wkv_sb = At("wkv_sb", [128, NFC * 512], BF16, 5 * KB)
    xa_sb = [At(f"xa_sb{i}", [128, NFC * 256], BF16, 37 * KB + i * 16 * KB) for i in range(2)]
    attn_mine = At("attn_mine", [128, 32 * 1024], BF16, 5 * KB)
    attnq_sb = At("attnq_sb", [128, 8 * 1024], BF16, 69 * KB)
    wstr_sb = [At(f"wstr_sb{i}", [128, NFC * 128], BF16, 69 * KB + i * 8 * KB) for i in range(2)]
    xb0_sb = At("xb0_sb", [128, NFC * 256], BF16, 69 * KB)
    out_stage = [At(f"out_stage{i}", [128, 1024], F32, 69 * KB + i * 4 * KB) for i in range(2)]
    exp_sb = [At(f"exp_sb{i}", [128, 1024], BF16, 85 * KB + i * 2 * KB) for i in range(2)]
    bcast_sb = [At(f"bcast_sb{i}", [128, 512], F32, 89 * KB + i * 2 * KB) for i in range(2)]
    mean_bc = At("mean_bc", [128, 1024], F32, 85 * KB)
    rstd_bc = At("rstd_bc", [128, 1024], F32, 89 * KB)
    ge0_sb = At("ge0_sb", [128, 1024], F32, 85 * KB)
    ge1_sb = At("ge1_sb", [96, 1024], F32, 89 * KB)
    m2_sb = At("m2_sb", [1, 1024], F32, 85 * KB)
    sums_sb = At("sums_sb", [1, 512], F32, 145 * KB)
    lntmp_sb = At("lntmp_sb", [1, 512], F32, 151 * KB)
    recip_pool = [At(f"recip_sb{i}", [1, 512], F32, 147 * KB + i * 2 * KB) for i in range(2)]
    exp2_sb = [At(f"exp2_sb{i}", [128, 1024], BF16, 93 * KB + i * 2 * KB) for i in range(2)]
    hg0_sb = At("hg0_sb", [128, 1024], BF16, 93 * KB)
    hg1_sb = At("hg1_sb", [96, 1024], BF16, 95 * KB)
    k_sb = [At(f"k_sb{p}", [128, 4096], BF16, 97 * KB + p * 8 * KB) for p in range(2)]
    v_sb = [At(f"v_sb{p}", [128, 4096], BF16, 113 * KB + p * 8 * KB) for p in range(2)]
    wqh_sb = At("wqh_sb", [128, NFC * 512], BF16, 129 * KB)
    xm1_sb = At("xm1_sb", [128, 32 * 1024], BF16, 97 * KB)
    colacc_sb = [At(f"colacc{i}", [128, 512], F32, 129 * KB + i * 2 * KB) for i in range(4)]
    attnu_sb = [At(f"attnu{i}", [128, 512], BF16, 137 * KB + i * KB) for i in range(4)]
    wqh1b1 = At("wqh1b1", [128, 4096], BF16, 85 * KB)
    wqh1b2 = At("wqh1b2", [128, 4096], BF16, 179 * KB)
    attnq2_sb = At("attnq2", [128, 8 * 1024], BF16, 165 * KB)
    xb1_sb = At("xb1_sb", [128, NFC * 256], BF16, 161 * KB)
    wbt0_sb = At("wbt0_sb", [128, 4096], BF16, 161 * KB)
    wbt1_sb = At("wbt1_sb", [96, 4096], BF16, 169 * KB)
    pairsum_sb2 = [At(f"pairsum{i}", [128, 512], BF16, 141 * KB + i * KB) for i in range(4)]
    sq_sb = At("sq_sb", [128, 1024], BF16, 177 * KB)
    logits_fm = At("logits_fm", [28, 1024], F32, 179 * KB)
    logits_tm = At("logits_tm", [128, 224], F32, 183 * KB)
    gates_sb = At("gates_sb", [28, 1024], BF16, 184 * KB)
    gtmp = At("gtmp", [128, 64], F32, 186 * KB)
    gsc = At("gsc", [128, 8], F32, 186 * KB + 256)
    geq = At("geq", [128, 64], F32, 186 * KB + 512)
    gate_tm = At("gate_tm", [128, 28], F32, 186 * KB + 768)
    mean_sb = At("mean_sb", [1, 1024], F32, 179 * KB)
    msq_sb = At("msq_sb", [1, 1024], F32, 183 * KB)

    S0 = ctx.enter_context(nc.psum_tensor("S0", [128, 1024], F32))
    S1 = ctx.enter_context(nc.psum_tensor("S1", [128, 1024], F32))
    T0 = ctx.enter_context(nc.psum_tensor("T0", [128, 512], F32))
    T1 = ctx.enter_context(nc.psum_tensor("T1", [128, 512], F32))
    T2 = ctx.enter_context(nc.psum_tensor("T2", [128, 512], F32))
    T3 = ctx.enter_context(nc.psum_tensor("T3", [128, 512], F32))

    SEMS = {}
    for s in ["sldc", "swkv", "sxa0", "sxa1", "swq", "sxb0", "sxb1",
              "sws0", "sws1", "smisc", "sxm", "sdbg",
              "pe", "ve", "se", "po", "ag", "rb", "out0", "out1",
              "cc0", "cc1", "cc2", "cc3"]:
        SEMS[s] = ctx.enter_context(nc.semaphore(s))

    class _DictD(dict):
        def __init__(self, base):
            super().__init__(base)

        def __getitem__(self, k):
            if k is None:
                return 0
            return super().get(k, 0)

        def get(self, k, d=0):
            if k is None:
                return d
            return super().get(k, d)

    au_prev = {}
    queues = {k: [] for k in ("tensor", "scalar", "vector", "sync", "gpsimd")}
    cnt = {s: 0 for s in SEMS}

    def emit(eng, fn):
        queues[eng].append(fn)

    def wait(eng, sem, val):
        if val > 0:
            emit(eng, lambda e, s=sem, v=val: e.wait_ge(SEMS[s], v))

    def eminc(eng, fn, sem, n=1):
        def wrapped(e, _fn=fn, _s=sem, _n=n):
            _fn(e).then_inc(SEMS[_s], _n)
        emit(eng, wrapped)
        cnt[sem] += n
        return cnt[sem]

    def sld(out_ap, in_ap):
        return eminc("sync", lambda e, o=out_ap, i=in_ap: e.dma_start(out=o, in_=i), "sldc", 16)

    def gld(sem, out_ap, in_ap):
        return eminc("gpsimd", lambda e, o=out_ap, i=in_ap: e.dma_start(out=o, in_=i), sem, 16)

    def mm(bank_ap, lhsT_ap, rhs_ap, start, stop, sem=None):
        fn = (lambda e, o=bank_ap, l=lhsT_ap, r=rhs_ap, s=start, st=stop:
              e.matmul(o, l, r, start=s, stop=st, skip_group_check=True))
        if sem is None:
            emit("tensor", lambda e, _f=fn: _f(e))
            return None
        return eminc("tensor", fn, sem)

    sld(identf_sb[:, :], identf)
    sld(ones_sb[:, 0:1], onescol)
    sld(onesrow_sb[:, :], onesrow)
    sld(mask_sb[:, :], maskdiag)
    sld(emat_sb[:, :], emat)
    sld(idx_sb[:, :], idx_rb)
    SLD_CONST = cnt["sldc"]

    emit("vector", lambda e: e.memset(cpool[:, 0:1], SCALE_ATT))
    emit("vector", lambda e: e.memset(cpool[:, 1:2], -0.5))
    emit("vector", lambda e: e.memset(cpool[:, 2:3], EPS))
    emit("vector", lambda e: e.memset(cpool[:, 4:5], 1.0))
    ve_cpool = eminc("vector", lambda e: e.memset(cpool[:, 5:6], -1.0), "ve")

    gld_wkv = gld("swkv", wkv_sb[:, :], wkv_t)
    xa_slots = [xa_sb[0], xa_sb[1], xb0_sb]
    XA_SEM = ["sxa0", "sxa1", "sxb0"]
    pe_tileA, ve_evacA, gld_xa = {}, {}, {}
    for tt in range(3):
        gld_xa[tt] = gld(XA_SEM[tt % 3], xa_slots[tt][:, :],
                         x_t[tt].rearrange("p c t -> p (c t)"))

    for tt in range(TTA):
        xt = xa_slots[tt % 3]
        kb0, kb1, vb = (T0, T1, S0) if tt % 2 == 0 else (T2, T3, S1)
        wait("tensor", "swkv", gld_wkv)
        wait("tensor", XA_SEM[tt % 3], gld_xa[tt])
        if tt >= 2:
            wait("tensor", "ve", ve_evacA[tt - 2])
        for p in range(2):
            kbank = kb0 if p == 0 else kb1
            for fc in range(NFC):
                mm(kbank[:, 0:256],
                   wkv_sb[:, fc * 512 + p * 128: fc * 512 + (p + 1) * 128],
                   xt[:, fc * 256:(fc + 1) * 256], fc == 0, fc == NFC - 1)
        for kt in range(2):
            off = 512 * kt
            for fc in range(NFC):
                mm(vb[:, off: off + 256],
                   xt[:, fc * 256 + kt * 128: fc * 256 + kt * 128 + 128],
                   wkv_sb[:, fc * 512 + 256: fc * 512 + 512], fc == 0, fc == NFC - 1,
                   sem="pe" if (kt == 1 and fc == NFC - 1) else None)
        pe_tileA[tt] = cnt["pe"]

        if tt + 3 < TTA:
            wait("gpsimd", "pe", pe_tileA[tt])
            gld_xa[tt + 3] = gld(XA_SEM[tt % 3], xa_slots[tt % 3][:, :],
                                 x_t[tt + 3].rearrange("p c t -> p (c t)"))
        if tt == 6:
            gld_wqh0 = gld("swq", wqh_sb[:, :], wq_t[0])

        wait("vector", "pe", pe_tileA[tt])
        for p in range(2):
            kbank = kb0 if p == 0 else kb1
            emit("vector", lambda e, o=k_sb[p][:, tt * 256:(tt + 1) * 256], i=kbank[:, 0:256]:
                 e.tensor_copy(o, i))
        for kt in range(2):
            off = 512 * kt
            ktg = tt * 2 + kt
            for p in range(2):
                fn = (lambda e, o=v_sb[p][:, ktg * 128:(ktg + 1) * 128],
                      i=vb[:, off + p * 128: off + (p + 1) * 128]: e.tensor_copy(o, i))
                if kt == 1 and p == 1:
                    eminc("vector", fn, "ve")
                else:
                    emit("vector", fn)
        ve_evacA[tt] = cnt["ve"]

    VE_A_DONE = ve_evacA[TTA - 1]
    if DEBUG:
        wait("sync", "ve", VE_A_DONE)
        for p in range(2):
            eminc("sync", lambda e, o=dbg_k[p], i=k_sb[p][:, :]:
                  e.dma_start(out=o, in_=i), "sdbg", 16)
            eminc("sync", lambda e, o=dbg_v[p], i=v_sb[p][:, :]:
                  e.dma_start(out=o, in_=i), "sdbg", 16)

    pe_tileB, ve_evacB = {}, {}
    QB_BANKS_EVEN = [(T0, 0), (T1, 0), (T2, 0), (T3, 0)]
    QB_BANKS_ODD = [(S0, 0), (S0, 512), (S1, 0), (S1, 512)]
    xb_sb = [xb0_sb, xb1_sb]
    gld_wqh, gld_xb = {}, {}

    for p in range(2):
        if p == 0:
            gld_wqh[0] = gld_wqh0
        else:
            wait("gpsimd", "pe", pe_tileB[(0, TTA - 1)])
            wait("gpsimd", "swq", gld_wqh1b2)
            gld_wqh1c = gld("swq", wqh_sb[:, 0:8192], wq_t[1][:, 8192:16384])
            gld_wqh[p] = max(gld_wqh1b1, gld_wqh1b2, gld_wqh1c)
        for tt in range(2):
            if p == 0 and tt == 0:
                wait("gpsimd", "pe", pe_tileA[14])
            gld_xb[(p, tt)] = gld(f"sxb{tt % 2}", xb_sb[tt][:, :],
                                  x_t[tt].rearrange("p c t -> p (c t)"))
        for tt in range(TTA):
            xt = xb_sb[tt % 2]
            banks = QB_BANKS_EVEN if tt % 2 == 0 else QB_BANKS_ODD
            wait("tensor", "swq", gld_wqh[p] if p == 0 else max(gld_wqh1b1, gld_wqh1b2))
            wait("tensor", f"sxb{tt % 2}", gld_xb[(p, tt)])
            if p == 0 and tt < 2:
                wait("tensor", "ve", VE_A_DONE)
            if (p, tt) == (0, 0):
                pass
            prev = (p, tt - 2) if tt >= 2 else ((p - 1, TTA + tt - 2) if p == 1 else None)
            if prev is not None and prev in ve_evacB:
                wait("tensor", "ve", ve_evacB[prev])
            for fc in range(NFC):
                if p == 1 and tt == 0 and fc == 16:
                    wait("tensor", "swq", gld_wqh1c)
                for g in range(4):
                    bank, off = banks[g]
                    if p == 0:
                        lhs = wqh_sb[:, fc * 512 + g * 128: fc * 512 + (g + 1) * 128]
                    elif fc < 8:
                        lhs = wqh1b1[:, fc * 512 + g * 128: fc * 512 + (g + 1) * 128]
                    elif fc < 16:
                        lhs = wqh1b2[:, (fc - 8) * 512 + g * 128: (fc - 8) * 512 + (g + 1) * 128]
                    else:
                        lhs = wqh_sb[:, (fc - 16) * 512 + g * 128: (fc - 16) * 512 + (g + 1) * 128]
                    mm(bank[:, off: off + 256], lhs,
                       xt[:, fc * 256:(fc + 1) * 256], fc == 0, fc == NFC - 1,
                       sem="pe" if (g == 3 and fc == NFC - 1) else None)
            pe_tileB[(p, tt)] = cnt["pe"]

            if tt + 2 < TTA:
                wait("gpsimd", "pe", pe_tileB[(p, tt)])
                gld_xb[(p, tt + 2)] = gld(f"sxb{tt % 2}", xb_sb[tt % 2][:, :],
                                          x_t[tt + 2].rearrange("p c t -> p (c t)"))
            if p == 0 and tt == 10:
                gld_wqh1b1 = gld("swq", wqh1b1[:, :], wq_t[1][:, 0:4096])
            if p == 0 and tt == 12:
                wait("gpsimd", "swq", gld_wqh1b1)
                gld_wqh1b2 = gld("swq", wqh1b2[:, :], wq_t[1][:, 4096:8192])

            wait("vector", "pe", pe_tileB[(p, tt)])
            for g in range(4):
                bank, off = banks[g]
                fn = (lambda e, o=q_sb[p][:, g * 4096 + tt * 256: g * 4096 + (tt + 1) * 256],
                      i=bank[:, off: off + 256]: e.tensor_copy(o, i))
                if g == 3:
                    eminc("vector", fn, "ve")
                else:
                    emit("vector", fn)
            ve_evacB[(p, tt)] = cnt["ve"]

    VE_B_DONE = ve_evacB[(1, TTA - 1)]
    if DEBUG:
        wait("sync", "ve", VE_B_DONE)
        for p in range(2):
            eminc("sync", lambda e, o=dbg_q[p], i=q_sb[p][:, 0:4096]:
                  e.dma_start(out=o, in_=i), "sdbg", 16)

    attnq_bufs = [attnq_sb, attnq2_sb]
    QORDER = [3, 2, 1, 0]

    sbank_reader_se = {0: 0, 1: 0}
    expbuf_pe = {}
    expbuf_ve = {}
    expbuf_po = {}
    pairsum_po = {0: 0, 1: 0, 2: 0, 3: 0}
    pairsum_po_also = {}
    pend_ps = {0: None, 1: None}
    group_pool_started = {0: False, 1: False}
    last_pair_of_group = [False]
    pe_sums, pe_bcast, se_bcast, se_rec = {}, {}, {}, {}
    ve_rec, ve_evacA_attn, ve_norm2 = {}, {}, {}
    po_group = {}
    EXPB = {0: exp_sb, 1: exp2_sb}
    SBANK = {0: S0, 1: S1}
    ABANK = {0: T0, 1: T1}

    def gkey(qoi, s, ml):
        return qoi * 16 + s * 8 + ml

    wait("tensor", "ve", max(VE_B_DONE, ve_cpool))
    wait("tensor", "sldc", SLD_CONST)
    wait("vector", "sldc", SLD_CONST)
    wait("vector", "ve", ve_cpool)
    wait("scalar", "ve", ve_cpool)

    def emit_scores(s, g, j, kts, m, rhs_q):
        wait("tensor", "se", sbank_reader_se[s])
        for (jj, kk) in enumerate(kts):
            mm(SBANK[s][:, jj * 512:(jj + 1) * 512],
               k_sb[s][:, kk * 128:(kk + 1) * 128], rhs_q, True, True,
               sem="pe" if jj == len(kts) - 1 else None)
        return cnt["pe"]

    def emit_exp_chain(s, g, j, kts, m, pe_scores, slot):
        wait("scalar", "pe", max(pe_scores, expbuf_pe.get((s, slot), 0)))
        wait("scalar", "ve", expbuf_ve.get((s, slot), 0))
        wait("scalar", "po", expbuf_po.get((s, slot), 0))
        ncols = 512 * len(kts)
        se_val = eminc("scalar",
                       lambda e, o=EXPB[s][slot][:, 0:ncols],
                       i=SBANK[s][:, 0:ncols], sc=cpool[:, 0:1]:
                       e.activation(o, i, AF.Exp, scale=sc), "se")
        sbank_reader_se[s] = se_val
        has_diag = kts[-1] == m
        wait("vector", "se", se_val)
        if has_diag:
            doff = 512 * (len(kts) - 1)
            vm = eminc("vector", lambda e, o=EXPB[s][slot][:, doff:doff + 512],
                       mk=mask_sb[:, :]: e.tensor_tensor(o, o, mk, ALU.mult), "ve")
            wait("vector", "ve", vm)
            att_wait = ("ve", vm)
        else:
            att_wait = ("se", se_val)
        ca = s * 2 + (g % 2)
        first_pool = (j <= 1)
        if len(kts) == 2:
            ps_slot = s * 2 + (j % 2)
            wait("vector", "po", pairsum_po[ps_slot])
            ve_ps = eminc("vector",
                          lambda e, o=pairsum_sb2[ps_slot][:, :], a=EXPB[s][slot][:, 0:512],
                          b=EXPB[s][slot][:, 512:1024]: e.tensor_tensor(o, a, b, ALU.add), "ve")
            expbuf_ve[(s, slot)] = ve_ps
            if j % 2 == 0 and not last_pair_of_group[0]:
                pend_ps[s] = (ps_slot, ve_ps)
                return att_wait
            if j % 2 == 1:
                pslot0, ve0 = pend_ps[s]
                pend_ps[s] = None
                wait("vector", "ve", max(ve0, ve_ps))
                ve_ps = eminc("vector",
                              lambda e, o=pairsum_sb2[ps_slot][:, :],
                              a=pairsum_sb2[pslot0][:, :]:
                              e.tensor_tensor(o, o, a, ALU.add), "ve")
                wait("vector", "ve", ve_ps)
                pairsum_po_also[ps_slot] = pslot0
            pool_src = pairsum_sb2[ps_slot][:, :]
            wait("gpsimd", "ve", ve_ps)
        else:
            pool_src = EXPB[s][slot][:, 0:512]
            wait("gpsimd", att_wait[0], att_wait[1])
            if pend_ps[s] is not None:
                pslot0, ve0 = pend_ps[s]
                pend_ps[s] = None
                if first_pool:
                    wait("gpsimd", "pe", pe_sums.get(gkey_prev2(s, g), 0))
                    first_pool = False
                wait("gpsimd", "ve", ve0)
                if group_pool_started[s]:
                    fnp = (lambda e, o=colacc_sb[ca][:, :], i=pairsum_sb2[pslot0][:, :]:
                           e.tensor_tensor(o, o, i, ALU.add))
                else:
                    fnp = (lambda e, o=colacc_sb[ca][:, :], i=pairsum_sb2[pslot0][:, :]:
                           e.tensor_copy(o, i))
                    group_pool_started[s] = True
                po_p = eminc("gpsimd", fnp, "po")
                wait("gpsimd", "po", po_p)
                pairsum_po[pslot0] = po_p
        if first_pool:
            wait("gpsimd", "pe", pe_sums.get(gkey_prev2(s, g), 0))
        if group_pool_started[s]:
            fn = (lambda e, o=colacc_sb[ca][:, :], i=pool_src: e.tensor_tensor(o, o, i, ALU.add))
        else:
            fn = (lambda e, o=colacc_sb[ca][:, :], i=pool_src: e.tensor_copy(o, i))
            group_pool_started[s] = True
        po_v = eminc("gpsimd", fn, "po")
        wait("gpsimd", "po", po_v)
        if len(kts) == 2:
            ps_slot = s * 2 + (j % 2)
            pairsum_po[ps_slot] = po_v
            other = pairsum_po_also.pop(ps_slot, None)
            if other is not None:
                pairsum_po[other] = po_v
        else:
            expbuf_po[(s, slot)] = po_v
        return att_wait

    def emit_attn2(s, g, j, kts, m, slot, att_wait):
        wait("tensor", att_wait[0], att_wait[1])
        if kts[0] == 0:
            wait("tensor", "ve", ve_evacA_attn.get(gkey_prev1(s, g), 0))
        for (jj, kk) in enumerate(kts):
            mm(ABANK[s][:, 0:512], v_sb[s][:, kk * 128:(kk + 1) * 128],
               EXPB[s][slot][:, jj * 512:(jj + 1) * 512], kk == 0, kk == m,
               sem="pe" if jj == len(kts) - 1 else None)
        expbuf_pe[(s, slot)] = cnt["pe"]
        return cnt["pe"]

    seq_of_stream = {0: [], 1: []}

    def gkey_prev1(s, g):
        lst = seq_of_stream[s]
        return lst[-1] if lst else None

    def gkey_prev2(s, g):
        lst = seq_of_stream[s]
        return lst[-2] if len(lst) >= 2 else None

    _ps, _van2, _vea = pe_sums, ve_norm2, ve_evacA_attn
    pe_sums = _DictD(_ps)
    ve_norm2 = _DictD(_van2)
    ve_evacA_attn = _DictD(_vea)

    def pairs_of(m):
        kts, out, kt = m + 1, [], 0
        while kt < kts:
            ks = [kt] if kt + 1 >= kts else [kt, kt + 1]
            out.append(ks)
            kt += len(ks)
        return out

    tail_sums = []
    tail_norm = []

    last_sums = [None]
    rec_slot = {}
    rec_ctr = [0]

    def emit_tail_sums():
        while tail_sums:
            gk, s, po_done = tail_sums.pop(0)
            wait("tensor", "po", po_done)
            if last_sums[0] is not None:
                wait("tensor", "se", ve_rec.get(last_sums[0], 0))
            last_sums[0] = gk
            ca = s * 2 + (sums_meta[gk] % 2)
            pe_sums[gk] = mm(T2[0:1, 0:512], cpool[:, 4:5], colacc_sb[ca][:, :],
                             True, True, sem="pe")
            rs = rec_ctr[0] % 2
            rec_slot[gk] = rs
            rec_ctr[0] += 1
            wait("scalar", "pe", pe_sums[gk])
            prevprev = rec_user.get(rs)
            if prevprev is not None:
                wait("scalar", "pe", pe_bcast.get(prevprev, 0))
            rec_user[rs] = gk
            se_l = eminc("scalar",
                         lambda e: e.activation(lntmp_sb[:, :], T2[0:1, 0:512], AF.Ln), "se")
            wait("scalar", "se", se_l)
            se_rec[gk] = eminc("scalar",
                               lambda e, o=recip_pool[rs][:, :]:
                               e.activation(o, lntmp_sb[:, :], AF.Exp,
                                            scale=cpool[0:1, 5:6]), "se")
            ve_rec[gk] = se_l

    last_bcast = [None]
    bc_slot = {}
    bc_ctr = [0]

    def emit_tail_norm():
        while tail_norm:
            gk, s, qoi_, ml_ = tail_norm.pop(0)
            aqbuf = attnq_bufs[qoi_ % 2]
            wait("tensor", "se", se_rec[gk])
            if last_bcast[0] is not None:
                wait("tensor", "se", se_bcast.get(last_bcast[0], 0))
            last_bcast[0] = gk
            bs = bc_ctr[0] % 2
            bc_slot[gk] = bs
            bc_ctr[0] += 1
            pe_bcast[gk] = mm(T3[:, 0:512], onesrow_sb[:, :], recip_pool[rec_slot[gk]][:, :],
                              True, True, sem="pe")
            wait("scalar", "pe", pe_bcast[gk])
            prevprev = bc_user.get(bs)
            if prevprev is not None:
                wait("scalar", "ve", ve_norm2.get(prevprev, 0))
            bc_user[bs] = gk
            se_bcast[gk] = eminc("scalar",
                                 lambda e, o=bcast_sb[bs][:, :], i=T3[:, 0:512]:
                                 e.copy(o, i), "se")
            wait("vector", "se", se_bcast[gk])
            if ml_ == 0 and qoi_ >= 2 and s == 0:
                wait("vector", "ag", 16 * (qoi_ - 1))
            ua = au_of[gk]
            for g_ in range(4):
                ch = s * 4 + g_
                fn = (lambda e,
                      o=aqbuf[:, ch * 1024 + ml_ * 128: ch * 1024 + (ml_ + 1) * 128],
                      i0=attnu_sb[ua][:, g_ * 128:(g_ + 1) * 128],
                      i1=bcast_sb[bs][:, g_ * 128:(g_ + 1) * 128]:
                      e.tensor_tensor(o, i0, i1, ALU.mult))
                if g_ == 3:
                    ve_norm2[gk] = eminc("vector", fn, "ve")
                else:
                    emit("vector", fn)

    def quarter_store(qoi, Q):
        wait("sync", "ve", ve_norm2[store_gate[qoi]])
        wait("sync", "ag", 16 * qoi)
        eminc("sync", lambda e, o=agin[Q].rearrange("(c x) t -> x c t", x=128),
              i=attnq_bufs[qoi % 2].rearrange("x (c t) -> x c t", c=8):
              e.dma_start(out=o, in_=i), "ag", 16)
        wait("gpsimd", "ag", 16 * (qoi + 1))
        eminc("gpsimd", lambda e, i=agin[Q], o=agout[Q * 4096:(Q + 1) * 4096, :]:
              e.collective_compute("AllGather", ALU.bypass,
                                   replica_groups=[[0, 1, 2, 3], [4, 5, 6, 7]],
                                   ins=[i.opt()], outs=[o.opt()]), f"cc{Q}")

    sums_order, bcast_order = [], []
    rec_user, bc_user = {}, {}
    sums_meta, au_of = {}, {}
    store_gate = {}
    pending_store = None
    au_rot = 0

    for qoi, Q in enumerate(QORDER):
        for ml in range(8):
            m = 8 * Q + ml
            pairs = pairs_of(m)
            J = len(pairs)
            rhs = {s: q_sb[s].rearrange("x (g t) -> x g t", g=4)[:, :, m * 128:(m + 1) * 128]
                   for s in range(2)}
            gks = {s: gkey(qoi, s, ml) for s in range(2)}
            for s in range(2):
                sums_meta[gks[s]] = len(seq_of_stream[s])
            att_waits = {}
            pe_sc = {}
            group_pool_started[0] = False
            group_pool_started[1] = False
            for j, kts in enumerate(pairs):
                slot = j % 2
                last_pair_of_group[0] = (j == J - 1)
                for s in range(2):
                    pe_sc[s] = emit_scores(s, gks[s], j, kts, m, rhs[s])
                if j == 0:
                    emit_tail_sums()
                if j == 1 or (J == 1 and j == 0):
                    emit_tail_norm()
                    if pending_store is not None:
                        quarter_store(*pending_store)
                        pending_store = None
                for s in range(2):
                    att_waits[(s, slot)] = emit_exp_chain(s, gks[s], j, kts, m, pe_sc[s], slot)
                if j > 0:
                    pslot = (j - 1) % 2
                    for s in range(2):
                        emit_attn2(s, gks[s], j - 1, pairs[j - 1], m, pslot,
                                   att_waits[(s, pslot)])
            pslot = (J - 1) % 2
            pe_att_end = {}
            for s in range(2):
                pe_att_end[s] = emit_attn2(s, gks[s], J - 1, pairs[J - 1], m, pslot,
                                           att_waits[(s, pslot)])
            for s in range(2):
                gk = gks[s]
                ua = au_rot % 4
                au_rot += 1
                au_of[gk] = ua
                wait("vector", "pe", pe_att_end[s])
                wait("vector", "ve", ve_norm2.get(au_prev.get(ua), 0) if True else 0)
                fn = (lambda e, o=attnu_sb[ua][:, :], i=ABANK[s][:, 0:512]:
                      e.tensor_copy(o, i))
                ve_evacA_attn[gk] = eminc("vector", fn, "ve")
                au_prev[ua] = gk
                tail_sums.append((gk, s, cnt["po"]))
                tail_norm.append((gk, s, qoi, ml))
                sums_order.append(gk)
                bcast_order.append(gk)
                seq_of_stream[s].append(gk)
            if ml == 7:
                store_gate[qoi] = gks[1]
                pending_store = (qoi, Q)

    emit_tail_sums()
    emit_tail_norm()
    quarter_store(*pending_store)
    pending_store = None

    if DEBUG:
        wait("sync", "ve", ve_norm2[store_gate[3]])
        eminc("sync", lambda e, o=dbg_aq, i=attnq_bufs[1][:, :]: e.dma_start(out=o, in_=i),
              "sdbg", 16)
        eminc("sync", lambda e: e.dma_start(out=dbg_rc, in_=recip_pool[0][:, :]), "sdbg", 16)
        for i_ in range(2):
            eminc("sync", lambda e, o=dbg_ca[i_], i=colacc_sb[i_][:, :]: e.dma_start(out=o, in_=i), "sdbg", 16)
            eminc("sync", lambda e, o=dbg_bc[i_], i=bcast_sb[i_][:, :]: e.dma_start(out=o, in_=i), "sdbg", 16)

    gld_wd = {}
    gld_wd[0] = gld("sws0", wstr_sb[0][:, :], wd_t[0])
    gld_wd[1] = gld("sws1", wstr_sb[1][:, :], wd_t[1])
    for q in range(4):
        wait("gpsimd", f"cc{q}", 1)
    wait("gpsimd", "sldc", SLD_CONST)
    rb_vals = {}
    gld_xm = {}
    for g in range(4):
        if g > 0:
            wait("gpsimd", "rb", 16 * 8 * g)
        if g == 1:
            gld_xm[0] = gld("sxm", xm1_sb[:, 0:16 * 1024],
                            xm_t.rearrange("p c t -> p (c t)")[:, 0:16 * 1024])
        for i_ in range(8):
            ic = g * 8 + i_
            eminc("gpsimd",
                  lambda e, o=attn_mine[:, ic * 1024:(ic + 1) * 1024],
                  i=agout[:, :], ix=idx_sb[:, ic:ic + 1]:
                  e.indirect_dma_start(out=o, out_offset=None, in_=i,
                                       in_offset=bass.IndirectOffsetOnAxis(ix, 0)),
                  "rb", 16)
    for ic in range(32):
        rb_vals[ic] = 16 * 8 * min(4, (ic // 8) + 1)
    wait("gpsimd", "sxm", 16)
    gld_xm[1] = gld("sxm", xm1_sb[:, 16 * 1024:32 * 1024],
                    xm_t.rearrange("p c t -> p (c t)")[:, 16 * 1024:32 * 1024])


    if DEBUG:
        wait("sync", "rb", cnt["rb"])
        eminc("sync", lambda e, o=dbg_am, i=attn_mine[:, :]: e.dma_start(out=o, in_=i),
              "sdbg", 16)

    def stats_mms(oc, src_sb, last_sem=None):
        mm(T0[0:1, 0:512], ones_sb[:, 0:1], src_sb[:, oc * 1024: oc * 1024 + 512],
           oc == 0, oc == 31)
        mm(T1[0:1, 0:512], ones_sb[:, 0:1], src_sb[:, oc * 1024 + 512:(oc + 1) * 1024],
           oc == 0, oc == 31)
        mm(T2[0:1, 0:512], ones_sb[:, 0:1], sq_sb[:, 0:512], oc == 0, oc == 31)
        return mm(T3[0:1, 0:512], ones_sb[:, 0:1], sq_sb[:, 512:1024], oc == 0, oc == 31,
                  sem=last_sem if last_sem else "pe")

    def ln_math_and_bcast(pe_stats_done):
        wait("vector", "pe", pe_stats_done)

        def vstep(fn):
            v = eminc("vector", fn, "ve")
            wait("vector", "ve", v)
            return v
        vstep(lambda e: e.tensor_copy(mean_sb[:, 0:512], T0[0:1, 0:512]))
        vstep(lambda e: e.tensor_copy(mean_sb[:, 512:1024], T1[0:1, 0:512]))
        vstep(lambda e: e.tensor_copy(msq_sb[:, 0:512], T2[0:1, 0:512]))
        vstep(lambda e: e.tensor_copy(msq_sb[:, 512:1024], T3[0:1, 0:512]))
        vstep(lambda e: e.tensor_scalar(mean_sb[:, :], mean_sb[:, :],
                                        1.0 / 4096.0, None, ALU.mult))
        vstep(lambda e: e.tensor_scalar(msq_sb[:, :], msq_sb[:, :],
                                        1.0 / 4096.0, None, ALU.mult))
        vstep(lambda e: e.tensor_tensor(m2_sb[:, :], mean_sb[:, :],
                                        mean_sb[:, :], ALU.mult))
        ve_var = vstep(lambda e: e.tensor_tensor(msq_sb[:, :], msq_sb[:, :],
                                                 m2_sb[:, :], ALU.subtract))
        wait("scalar", "ve", ve_var)
        se_ln = eminc("scalar", lambda e: e.activation(msq_sb[:, :], msq_sb[:, :], AF.Ln,
                                                       bias=cpool[0:1, 2:3]), "se")
        wait("scalar", "se", se_ln)
        se_rstd = eminc("scalar", lambda e: e.activation(msq_sb[:, :], msq_sb[:, :], AF.Exp,
                                                         scale=cpool[0:1, 1:2]), "se")
        wait("tensor", "se", se_rstd)
        mm(S0[:, 0:512], onesrow_sb[:, :], mean_sb[:, 0:512], True, True)
        mm(S0[:, 512:1024], onesrow_sb[:, :], mean_sb[:, 512:1024], True, True)
        mm(S1[:, 0:512], onesrow_sb[:, :], msq_sb[:, 0:512], True, True)
        pe_bc = mm(S1[:, 512:1024], onesrow_sb[:, :], msq_sb[:, 512:1024], True, True,
                   sem="pe")
        wait("scalar", "pe", pe_bc)
        emit("scalar", lambda e: e.copy(mean_bc[:, :], S0[:, :]))
        se_cp = eminc("scalar", lambda e: e.copy(rstd_bc[:, :], S1[:, :]), "se")
        return se_cp

    pe_oc, ve_add, se_sq, pe_stats = {}, {}, {}, {}
    for oc in range(32):
        bank = S0 if oc % 2 == 0 else S1
        wait("tensor", f"sws{oc % 2}", gld_wd[oc])
        if oc == 0:
            for q in range(4):
                wait("tensor", f"cc{q}", 1)
        if oc >= 2:
            wait("tensor", "ve", ve_add[oc - 2])
        for half in range(2):
            for ic in range(NFC):
                if oc == 0 and half == 0:
                    wait("tensor", "rb", rb_vals[ic])
                mm(bank[:, half * 512:(half + 1) * 512],
                   wstr_sb[oc % 2][:, ic * 128:(ic + 1) * 128],
                   attn_mine[:, ic * 1024 + half * 512: ic * 1024 + half * 512 + 512],
                   ic == 0, ic == NFC - 1,
                   sem="pe" if (half == 1 and ic == NFC - 1) else None)
        pe_oc[oc] = cnt["pe"]
        if oc >= 1:
            po = oc - 1
            wait("tensor", "ve", ve_add[po])
            wait("tensor", "se", se_sq[po])
            pe_stats[po] = stats_mms(po, xm1_sb)
        if oc + 2 < 32:
            wait("gpsimd", "pe", pe_oc[oc])
            gld_wd[oc + 2] = gld(f"sws{oc % 2}", wstr_sb[oc % 2][:, :], wd_t[oc + 2])

        wait("vector", "pe", pe_oc[oc])
        if oc == 0:
            wait("vector", "sxm", 16)
        if oc == 16:
            wait("vector", "sxm", 32)
        for half in range(2):
            fn = (lambda e, o=xm1_sb[:, oc * 1024 + half * 512: oc * 1024 + half * 512 + 512],
                  i0=bank[:, half * 512:(half + 1) * 512]:
                  e.tensor_tensor(o, i0, o, ALU.add))
            if half == 1:
                ve_add[oc] = eminc("vector", fn, "ve")
            else:
                emit("vector", fn)
        wait("scalar", "ve", ve_add[oc])
        if oc >= 1:
            wait("scalar", "pe", pe_stats[oc - 1])
        se_sq[oc] = eminc("scalar",
                          lambda e, o=sq_sb[:, :], i=xm1_sb[:, oc * 1024:(oc + 1) * 1024]:
                          e.activation(o, i, AF.Square), "se")

    wait("tensor", "ve", ve_add[31])
    wait("tensor", "se", se_sq[31])
    pe_stats[31] = stats_mms(31, xm1_sb)
    se_ln1 = ln_math_and_bcast(pe_stats[31])

    ve_n1 = {}
    for oc in range(32):
        if oc == 0:
            wait("vector", "se", se_ln1)
        vv = eminc("vector", lambda e, o=xm1_sb[:, oc * 1024:(oc + 1) * 1024], mb=mean_bc[:, :]:
                   e.tensor_tensor(o, o, mb, ALU.subtract), "ve")
        wait("vector", "ve", vv)
        ve_n1[oc] = eminc("vector",
                          lambda e, o=xm1_sb[:, oc * 1024:(oc + 1) * 1024], rb=rstd_bc[:, :]:
                          e.tensor_tensor(o, o, rb, ALU.mult), "ve")

    if DEBUG:
        wait("sync", "ve", ve_n1[31])
        eminc("sync", lambda e, o=dbg_x1, i=xm1_sb[:, :]: e.dma_start(out=o, in_=i),
              "sdbg", 16)

    gld_wg = gld("smisc", wg_sb[:, :], wg_t)
    gld_wa = {}
    wait("gpsimd", "pe", pe_oc[31])
    gld_wa[0] = gld("sws0", wstr_sb[0][:, 0:16 * ER], wa_t[0])
    gld_wa[1] = gld("sws1", wstr_sb[1][:, 0:16 * ER], wa_t[1])
    gld_wbt0 = gld("smisc", wbt0_sb[:, :], wbt0_t)
    gld_wbt1 = gld("smisc", wbt1_sb[:, :], wbt1_t)
    SMISC_TOTAL = cnt["smisc"]

    wait("tensor", "smisc", SMISC_TOTAL)
    for half in range(2):
        gb = T0 if half == 0 else T1
        for fc in range(NFC):
            if half == 0:
                wait("tensor", "ve", ve_n1[fc])
            mm(gb[0:28, 0:512], wg_sb[:, fc * 28:(fc + 1) * 28],
               xm1_sb[:, fc * 1024 + half * 512: fc * 1024 + half * 512 + 512],
               fc == 0, fc == NFC - 1,
               sem="pe" if (half == 1 and fc == NFC - 1) else None)
    pe_gate = cnt["pe"]
    wait("vector", "pe", pe_gate)
    emit("vector", lambda e: e.tensor_copy(logits_fm[:, 0:512], T0[0:28, 0:512]))
    ve_logits = eminc("vector",
                      lambda e: e.tensor_copy(logits_fm[:, 512:1024], T1[0:28, 0:512]), "ve")

    wait("tensor", "ve", ve_logits)
    for t in range(8):
        fn = (lambda e, o=T2[:, t * 28:(t + 1) * 28], i=logits_fm[0:28, t * 128:(t + 1) * 128],
              iden=identf_sb[0:28, 0:28]: e.transpose(o, i, iden))
        if t == 7:
            pe_tr1 = eminc("tensor", fn, "pe")
        else:
            emit("tensor", fn)
    wait("vector", "pe", pe_tr1)
    ve_ltm = eminc("vector", lambda e: e.tensor_copy(logits_tm[:, :], T2[:, 0:224]), "ve")

    h_sched = []
    for half in range(2):
        for fc in range(NFC):
            wb_ = wstr_sb[fc // 16]
            fo = (fc % 16) * ER
            h_sched.append((S0[:, half * 512:(half + 1) * 512], wb_[:, fo: fo + 128],
                            xm1_sb[:, fc * 1024 + half * 512: fc * 1024 + half * 512 + 512],
                            fc == 0, fc == NFC - 1))
    for half in range(2):
        for fc in range(NFC):
            wb_ = wstr_sb[fc // 16]
            fo = (fc % 16) * ER
            h_sched.append((S1[0:96, half * 512:(half + 1) * 512], wb_[:, fo + 128: fo + 224],
                            xm1_sb[:, fc * 1024 + half * 512: fc * 1024 + half * 512 + 512],
                            fc == 0, fc == NFC - 1))
    h_pos = [0]

    def emit_h(n):
        for (o, l, r, s, st) in h_sched[h_pos[0]: h_pos[0] + n]:
            mm(o, l, r, s, st)
        h_pos[0] += n

    pe_tr2 = {}
    for t in range(8):
        lt = logits_tm[:, t * 28:(t + 1) * 28]
        d1, masked = gtmp[:, 0:28], gtmp[:, 32:60]
        eq1, eq2 = geq[:, 0:28], geq[:, 32:60]
        m1v, m2v, ee, ee2, uu2, uu, p2v = (gsc[:, 0:1], gsc[:, 1:2], gsc[:, 2:3],
                                           gsc[:, 3:4], gsc[:, 4:5], gsc[:, 5:6],
                                           gsc[:, 6:7])

        def vstep(fn):
            v = eminc("vector", fn, "ve")
            wait("vector", "ve", v)
            return v
        if t == 0:
            wait("vector", "ve", ve_ltm)
        else:
            wait("vector", "pe", pe_tr2[t - 1])
        vstep(lambda e, lt=lt, m1=m1v: e.reduce_max(m1, lt, axis=AX.X))
        vstep(lambda e, lt=lt, m1=m1v, d=d1: e.tensor_scalar(d, lt, m1, None, ALU.subtract))
        vstep(lambda e, d=d1, q=eq1: e.tensor_single_scalar(q, d, 0.0, ALU.is_ge))
        vstep(lambda e, q=eq1, mk=masked: e.tensor_scalar(mk, q, 1e30, None, ALU.mult))
        vstep(lambda e, lt=lt, mk=masked: e.tensor_tensor(mk, lt, mk, ALU.subtract))
        vstep(lambda e, mk=masked, m2=m2v: e.reduce_max(m2, mk, axis=AX.X))
        vstep(lambda e, mk=masked, m2=m2v, d=d1: e.tensor_scalar(d, mk, m2, None, ALU.subtract))
        vstep(lambda e, d=d1, q=eq2: e.tensor_single_scalar(q, d, 0.0, ALU.is_ge))
        ve_ee = vstep(lambda e, m1=m1v, m2=m2v, o=ee: e.tensor_tensor(o, m2, m1, ALU.subtract))
        wait("scalar", "ve", ve_ee)
        se_ee = eminc("scalar", lambda e, o=ee2, i=ee: e.activation(o, i, AF.Exp), "se")
        wait("vector", "se", se_ee)
        vstep(lambda e, o=uu2, i=ee2: e.tensor_scalar(o, i, 1.0, None, ALU.add))
        vstep(lambda e, o=uu, i=uu2: e.reciprocal(o, i))
        vstep(lambda e, o=p2v, a=ee2, b=uu: e.tensor_tensor(o, a, b, ALU.mult))
        vstep(lambda e, q=eq1, s=uu: e.tensor_scalar(q, q, s, None, ALU.mult))
        vstep(lambda e, q=eq2, s=p2v: e.tensor_scalar(q, q, s, None, ALU.mult))
        ve_gate_done = vstep(lambda e, a=eq1, b=eq2, o=gate_tm[:, :]:
                             e.tensor_tensor(o, a, b, ALU.add))
        bank = T3 if t < 4 else T2
        boff = (t % 4) * 128
        wait("tensor", "ve", ve_gate_done)
        if t == 4:
            wait("tensor", "ve", ve_ltm)
        pe_tr2[t] = eminc("tensor",
                          lambda e, o=bank[0:28, boff:boff + 128], i=gate_tm[:, :],
                          iden=identf_sb[:, :]: e.transpose(o, i, iden), "pe")
        if t == 0:
            wait("tensor", "sws0", gld_wa[0])
            wait("tensor", "sws1", gld_wa[1])
        emit_h(32)
        if t == 3:
            wait("vector", "pe", pe_tr2[3])
            emit("vector", lambda e: e.tensor_copy(gates_sb[0:28, 0:512], T3[0:28, 0:512]))
        if t == 7:
            wait("vector", "pe", pe_tr2[7])
            ve_gates = eminc("vector",
                             lambda e: e.tensor_copy(gates_sb[0:28, 512:1024], T2[0:28, 0:512]),
                             "ve")

    emit_h(len(h_sched) - h_pos[0])
    wait("tensor", "ve", ve_gates)
    for half in range(2):
        mm((T0 if half == 0 else T1)[:, 0:512], emat_sb[:, 0:128],
           gates_sb[0:28, half * 512:(half + 1) * 512], True, True)
        pe_hgx = mm((T2 if half == 0 else T3)[0:96, 0:512], emat_sb[:, 128:224],
                    gates_sb[0:28, half * 512:(half + 1) * 512], True, True, sem="pe")
    wait("scalar", "pe", pe_hgx)
    emit("scalar", lambda e: e.copy(ge0_sb[:, 0:512], T0[:, 0:512]))
    emit("scalar", lambda e: e.copy(ge0_sb[:, 512:1024], T1[:, 0:512]))
    emit("scalar", lambda e: e.copy(ge1_sb[0:96, 0:512], T2[0:96, 0:512]))
    se_ge = eminc("scalar", lambda e: e.copy(ge1_sb[0:96, 512:1024], T3[0:96, 0:512]), "se")
    wait("vector", "se", se_ge)
    emit("vector", lambda e: e.tensor_tensor(hg0_sb[:, 0:512], S0[:, 0:512], ge0_sb[:, 0:512], ALU.mult))
    emit("vector", lambda e: e.tensor_tensor(hg0_sb[:, 512:1024], S0[:, 512:1024], ge0_sb[:, 512:1024], ALU.mult))
    emit("vector", lambda e: e.tensor_tensor(hg1_sb[0:96, 0:512], S1[0:96, 0:512], ge1_sb[0:96, 0:512], ALU.mult))
    ve_hg = eminc("vector", lambda e: e.tensor_tensor(hg1_sb[0:96, 512:1024], S1[0:96, 512:1024],
                                                      ge1_sb[0:96, 512:1024], ALU.mult), "ve")

    gld_wb = {}
    wait("gpsimd", "pe", pe_hgx)
    gld_wb[0] = gld("sws0", wstr_sb[0][:, :], wb_t[0])
    gld_wb[1] = gld("sws1", wstr_sb[1][:, :], wb_t[1])
    pe_oc2, ve_add2, se_sq2, pe_stats2 = {}, {}, {}, {}
    for oc in range(32):
        bank = S0 if oc % 2 == 0 else S1
        wait("tensor", f"sws{oc % 2}", gld_wb[oc])
        if oc == 0:
            wait("tensor", "ve", ve_hg)
            wait("tensor", "se", se_ge)
        if oc >= 2:
            wait("tensor", "ve", ve_add2[oc - 2])
        for half in range(2):
            for fc in range(NFC):
                mm(bank[:, half * 512:(half + 1) * 512],
                   wstr_sb[oc % 2][:, fc * 128:(fc + 1) * 128],
                   xm1_sb[:, fc * 1024 + half * 512: fc * 1024 + half * 512 + 512],
                   fc == 0, False)
            mm(bank[:, half * 512:(half + 1) * 512], wbt0_sb[:, oc * 128:(oc + 1) * 128],
               hg0_sb[:, half * 512:(half + 1) * 512], False, False)
            mm(bank[:, half * 512:(half + 1) * 512], wbt1_sb[0:96, oc * 128:(oc + 1) * 128],
               hg1_sb[0:96, half * 512:(half + 1) * 512], False, True,
               sem="pe" if half == 1 else None)
        pe_oc2[oc] = cnt["pe"]
        if oc >= 1:
            po = oc - 1
            wait("tensor", "ve", ve_add2[po])
            wait("tensor", "se", se_sq2[po])
            pe_stats2[po] = stats_mms(po, attn_mine)
        if oc + 2 < 32:
            wait("gpsimd", "pe", pe_oc2[oc])
            gld_wb[oc + 2] = gld(f"sws{oc % 2}", wstr_sb[oc % 2][:, :], wb_t[oc + 2])

        wait("vector", "pe", pe_oc2[oc])
        for half in range(2):
            fn = (lambda e, o=attn_mine[:, oc * 1024 + half * 512: oc * 1024 + half * 512 + 512],
                  i0=bank[:, half * 512:(half + 1) * 512],
                  i1=xm1_sb[:, oc * 1024 + half * 512: oc * 1024 + half * 512 + 512]:
                  e.tensor_tensor(o, i0, i1, ALU.add))
            if half == 1:
                ve_add2[oc] = eminc("vector", fn, "ve")
            else:
                emit("vector", fn)
        wait("scalar", "ve", ve_add2[oc])
        if oc >= 1:
            wait("scalar", "pe", pe_stats2[oc - 1])
        se_sq2[oc] = eminc("scalar",
                           lambda e, o=sq_sb[:, :], i=attn_mine[:, oc * 1024:(oc + 1) * 1024]:
                           e.activation(o, i, AF.Square), "se")

    if DEBUG:
        wait("sync", "ve", ve_hg)
        eminc("sync", lambda e, o=dbg_hg, i=hg0_sb[:, :]: e.dma_start(out=o, in_=i), "sdbg", 16)
        wait("sync", "ve", ve_add2[31])
        eminc("sync", lambda e, o=dbg_x2, i=attn_mine[:, :]: e.dma_start(out=o, in_=i), "sdbg", 16)
    wait("tensor", "ve", ve_add2[31])
    wait("tensor", "se", se_sq2[31])
    pe_stats2[31] = stats_mms(31, attn_mine)
    se_ln2 = ln_math_and_bcast(pe_stats2[31])

    for oc in range(32):
        if oc == 0:
            wait("vector", "se", se_ln2)
        if oc >= 2:
            wait("vector", f"out{oc % 2}", 16 * (oc // 2))
        st = out_stage[oc % 2]
        vv = eminc("vector", lambda e, o=st[:, :], i=attn_mine[:, oc * 1024:(oc + 1) * 1024],
                   mb=mean_bc[:, :]: e.tensor_tensor(o, i, mb, ALU.subtract), "ve")
        wait("vector", "ve", vv)
        ve_o = eminc("vector", lambda e, o=st[:, :], rb=rstd_bc[:, :]:
                     e.tensor_tensor(o, o, rb, ALU.mult), "ve")
        wait("sync", "ve", ve_o)
        eminc("sync", lambda e, o=outT[oc * 128:(oc + 1) * 128, :], i=st[:, :]:
              e.dma_start(out=o, in_=i), f"out{oc % 2}", 16)
    wait("sync", "out0", cnt["out0"])
    wait("sync", "out1", cnt["out1"])
    if DEBUG:
        wait("sync", "sdbg", cnt["sdbg"])

    with nc.Block() as block:
        @block.tensor
        def _(e):
            for fn in queues["tensor"]:
                fn(e)

        @block.scalar
        def _(e):
            for fn in queues["scalar"]:
                fn(e)

        @block.vector
        def _(e):
            for fn in queues["vector"]:
                fn(e)

        @block.sync
        def _(e):
            for fn in queues["sync"]:
                fn(e)

        @block.gpsimd
        def _(e):
            for fn in queues["gpsimd"]:
                fn(e)

    ctx.__exit__(None, None, None)
    return nc


_NC_CACHE = [None]


def _prep_inputs(L0_attn_q, L0_attn_w_qkv, L0_attn_w_dense, L0_ffn_base_W,
                 L0_ffn_gate_W, L0_ffn_lora_A, L0_ffn_lora_B):
    bf = ml_dtypes.bfloat16
    x = np.ascontiguousarray(L0_attn_q, dtype=np.float32)
    wqkv = np.ascontiguousarray(L0_attn_w_qkv, dtype=np.float32)
    wdense = np.ascontiguousarray(L0_attn_w_dense, dtype=np.float32)
    baseW = np.ascontiguousarray(L0_ffn_base_W, dtype=np.float32)
    gateW = np.ascontiguousarray(L0_ffn_gate_W, dtype=np.float32)
    loraA = np.ascontiguousarray(L0_ffn_lora_A, dtype=np.float32)
    loraB = np.ascontiguousarray(L0_ffn_lora_B, dtype=np.float32)

    wd_t = np.ascontiguousarray(
        wdense.T.reshape(32, 128, 32, 128).transpose(2, 1, 0, 3).reshape(32, 128, 32 * 128))
    wb_t = np.ascontiguousarray(
        baseW.T.reshape(32, 128, 32, 128).transpose(2, 1, 0, 3).reshape(32, 128, 32 * 128))
    wg_t = np.ascontiguousarray(gateW.T.reshape(32, 128, 28).transpose(1, 0, 2).reshape(128, 32 * 28))
    a_flat = loraA.reshape(224, 4096)
    wa_full = np.ascontiguousarray(a_flat.T.reshape(32, 128, 224).transpose(1, 0, 2))
    wa_t = np.ascontiguousarray(
        wa_full.reshape(128, 2, 16, 224).transpose(1, 0, 2, 3).reshape(2, 128, 16 * 224))
    bt_flat = np.ascontiguousarray(loraB.transpose(0, 2, 1).reshape(224, 4096))
    wbt0_t = np.ascontiguousarray(bt_flat[0:128])
    wbt1_t = np.ascontiguousarray(bt_flat[128:224])

    identf = np.eye(128, dtype=np.float32)
    onescol = np.ones((128, 1), dtype=bf)
    onesrow = np.ones((1, 128), dtype=np.float32)
    kq = np.arange(128)
    mask = (kq[:, None] <= kq[None, :]).astype(bf)
    maskdiag = np.ascontiguousarray(np.tile(mask, (1, 4)))
    emat = np.zeros((28, 224), dtype=np.float32)
    for e_ in range(28):
        emat[e_, e_ * 8:(e_ + 1) * 8] = 2.0
    emat = emat.astype(bf)

    in_maps = []
    for c in range(8):
        b, r = c // 4, c % 4
        xT = np.ascontiguousarray(x[b].T)
        x_t = np.ascontiguousarray(
            xT.reshape(32, 128, 16, 256).transpose(2, 1, 0, 3).reshape(TTA, 128, 32, 256))
        xm = np.ascontiguousarray(
            xT[:, r * 1024:(r + 1) * 1024].reshape(32, 128, 1024).transpose(1, 0, 2))
        wq_h = []
        for p in range(2):
            kv = 2 * r + p
            rows = wqkv[(kv * 4) * 128:(kv * 4 + 4) * 128]
            wq_h.append(rows.T.reshape(32, 128, 4, 128).transpose(1, 0, 2, 3)
                        .reshape(128, 32 * 512))
        wq_t = np.ascontiguousarray(np.stack(wq_h))
        kvrows = []
        for p in range(2):
            kv = 2 * r + p
            kvrows.append(wqkv[4096 + kv * 128: 4096 + (kv + 1) * 128])
        for p in range(2):
            kv = 2 * r + p
            kvrows.append(wqkv[5120 + kv * 128: 5120 + (kv + 1) * 128])
        wkv = np.concatenate(kvrows, axis=0)
        wkv_t = np.ascontiguousarray(
            wkv.T.reshape(32, 128, 4, 128).transpose(1, 0, 2, 3).reshape(128, 32 * 512))
        ic = np.arange(32)
        rows = r * 4096 + (ic // 8) * 1024 + (ic % 8) * 128
        idx = (rows[None, :] + np.arange(128)[:, None]).astype(np.int32)
        in_maps.append({
            "x_t": x_t, "xm_t": xm, "wq_t": wq_t, "wkv_t": wkv_t,
            "wd_t": wd_t, "wb_t": wb_t, "wg_t": wg_t, "wa_t": wa_t,
            "wbt0_t": wbt0_t, "wbt1_t": wbt1_t,
            "identf": identf, "onescol": onescol, "onesrow": onesrow,
            "maskdiag": maskdiag, "emat": emat, "idx_rb": idx,
        })
    return in_maps


def kernel(L0_attn_q, L0_attn_w_qkv, L0_attn_w_dense, L0_ffn_base_W,
           L0_ffn_gate_W, L0_ffn_lora_A, L0_ffn_lora_B):
    in_maps = _prep_inputs(L0_attn_q, L0_attn_w_qkv, L0_attn_w_dense,
                           L0_ffn_base_W, L0_ffn_gate_W, L0_ffn_lora_A, L0_ffn_lora_B)
    if _NC_CACHE[0] is None:
        _NC_CACHE[0] = build_nc()
    nc = _NC_CACHE[0]
    res = run_bass_kernel_spmd(nc, in_maps, core_ids=list(range(8)))
    out = np.empty((2, 4096, 4096), dtype=np.float32)
    for c in range(8):
        b, r = c // 4, c % 4
        out[b, r * 1024:(r + 1) * 1024, :] = np.asarray(res.results[c]["outT"]).T
    return out


# version 19
# speedup vs baseline: 1.0799x; 1.0799x over previous
import contextlib
import numpy as np
import ml_dtypes
import concourse.bass as bass
import concourse.mybir as mybir
from concourse.bass_utils import run_bass_kernel_spmd

F32 = mybir.dt.float32
BF16 = mybir.dt.bfloat16
I32 = mybir.dt.int32
AF = mybir.ActivationFunctionType
ALU = mybir.AluOpType
AX = mybir.AxisListType

NFC = 32
TTA = 16
SCALE_ATT = float(1.0 / np.sqrt(128.0))
EPS = 1e-5
ER = 224
KB = 1024
DEBUG = False


def build_nc():
    nc = bass.Bass("TRN2", target_bir_lowering=False, debug=False)

    dram = lambda n, sh, ty, kind: nc.dram_tensor(n, list(sh), ty, kind=kind).ap()
    x_t = dram("x_t", (TTA, 128, NFC, 256), F32, "ExternalInput")
    xm_t = dram("xm_t", (128, NFC, 1024), F32, "ExternalInput")
    wq_t = dram("wq_t", (2, 128, NFC * 512), F32, "ExternalInput")
    wkv_t = dram("wkv_t", (128, NFC * 512), F32, "ExternalInput")
    wd_t = dram("wd_t", (NFC, 128, NFC * 128), F32, "ExternalInput")
    wb_t = dram("wb_t", (NFC, 128, NFC * 128), F32, "ExternalInput")
    wg_t = dram("wg_t", (128, NFC * 28), F32, "ExternalInput")
    wa_t = dram("wa_t", (2, 128, 16 * ER), F32, "ExternalInput")
    wbt0_t = dram("wbt0_t", (128, 4096), F32, "ExternalInput")
    wbt1_t = dram("wbt1_t", (96, 4096), F32, "ExternalInput")
    identf = dram("identf", (128, 128), F32, "ExternalInput")
    onescol = dram("onescol", (128, 1), BF16, "ExternalInput")
    onesrow = dram("onesrow", (1, 128), F32, "ExternalInput")
    maskdiag = dram("maskdiag", (128, 512), BF16, "ExternalInput")
    emat = dram("emat", (28, ER), BF16, "ExternalInput")
    idx_rb = dram("idx_rb", (128, 32), I32, "ExternalInput")
    outT = dram("outT", (4096, 1024), F32, "ExternalOutput")
    agin = [dram(f"agin{q}", (1024, 1024), BF16, "Internal") for q in range(4)]
    agout = dram("agout", (16384, 1024), BF16, "Internal")
    if DEBUG:
        dbg_k = dram("dbg_k", (2, 128, 4096), BF16, "ExternalOutput")
        dbg_v = dram("dbg_v", (2, 128, 4096), BF16, "ExternalOutput")
        dbg_q = dram("dbg_q", (2, 128, 4096), BF16, "ExternalOutput")
        dbg_aq = dram("dbg_aq", (128, 8192), BF16, "ExternalOutput")
        dbg_am = dram("dbg_am", (128, 32768), BF16, "ExternalOutput")
        dbg_x1 = dram("dbg_x1", (128, 32768), BF16, "ExternalOutput")
        dbg_x2 = dram("dbg_x2", (128, 32768), BF16, "ExternalOutput")
        dbg_gt = dram("dbg_gt", (28, 1024), BF16, "ExternalOutput")
        dbg_hg = dram("dbg_hg", (128, 1024), BF16, "ExternalOutput")
        dbg_lg = dram("dbg_lg", (128, 224), F32, "ExternalOutput")
        dbg_rc = dram("dbg_rc", (1, 512), F32, "ExternalOutput")
        dbg_ca = dram("dbg_ca", (2, 128, 512), F32, "ExternalOutput")
        dbg_bc = dram("dbg_bc", (2, 128, 512), F32, "ExternalOutput")

    ctx = contextlib.ExitStack()
    ctx.__enter__()

    ARENA = 187 * KB
    arena = nc.alloc_sbuf_tensor("arena", [128, ARENA], mybir.dt.uint8)
    base = nc.lookup_mloc(arena).addr

    def At(name, shape, ty, off):
        assert off + int(np.prod(shape[1:])) * mybir.dt.size(ty) <= ARENA, name
        return nc.alloc_sbuf_tensor_at(name, list(shape), ty, offset=base + off)

    cpool = At("cpool", [128, 16], F32, 0)
    ones_sb = At("ones_sb", [128, 32], BF16, 64)
    idx_sb = At("idx_sb", [128, 32], I32, 192)
    identf_sb = At("identf_sb", [128, 128], F32, 320)
    mask_sb = At("mask_sb", [128, 512], BF16, 832)
    onesrow_sb = At("onesrow_sb", [1, 128], F32, 1856)
    emat_sb = At("emat_sb", [28, ER], BF16, 2368)
    wg_sb = At("wg_sb", [128, NFC * 28], BF16, 2880)

    q_sb = [At(f"q_sb{p}", [128, 4 * 4096], BF16, 5 * KB + p * 32 * KB) for p in range(2)]
    wkv_sb = At("wkv_sb", [128, NFC * 512], BF16, 5 * KB)
    xa_sb = [At(f"xa_sb{i}", [128, NFC * 256], BF16, 37 * KB + i * 16 * KB) for i in range(2)]
    attn_mine = At("attn_mine", [128, 32 * 1024], BF16, 5 * KB)
    attnq_sb = At("attnq_sb", [128, 8 * 1024], BF16, 69 * KB)
    wstr_sb = [At(f"wstr_sb{i}", [128, NFC * 128], BF16, 69 * KB + i * 8 * KB) for i in range(2)]
    xb0_sb = At("xb0_sb", [128, NFC * 256], BF16, 69 * KB)
    out_stage = [At(f"out_stage{i}", [128, 1024], F32, 69 * KB + i * 4 * KB) for i in range(2)]
    exp_sb = [At(f"exp_sb{i}", [128, 1024], BF16, 85 * KB + i * 2 * KB) for i in range(2)]
    bcast_sb = [At(f"bcast_sb{i}", [128, 512], F32, 89 * KB + i * 2 * KB) for i in range(2)]
    mean_bc = At("mean_bc", [128, 1024], F32, 85 * KB)
    rstd_bc = At("rstd_bc", [128, 1024], F32, 89 * KB)
    ge0_sb = At("ge0_sb", [128, 1024], F32, 85 * KB)
    ge1_sb = At("ge1_sb", [96, 1024], F32, 89 * KB)
    m2_sb = At("m2_sb", [1, 1024], F32, 85 * KB)
    sums_sb = At("sums_sb", [1, 512], F32, 145 * KB)
    lntmp_sb = At("lntmp_sb", [1, 512], F32, 151 * KB)
    recip_pool = [At(f"recip_sb{i}", [1, 512], F32, 147 * KB + i * 2 * KB) for i in range(2)]
    exp2_sb = [At(f"exp2_sb{i}", [128, 1024], BF16, 93 * KB + i * 2 * KB) for i in range(2)]
    hg0_sb = At("hg0_sb", [128, 1024], BF16, 93 * KB)
    hg1_sb = At("hg1_sb", [96, 1024], BF16, 95 * KB)
    k_sb = [At(f"k_sb{p}", [128, 4096], BF16, 97 * KB + p * 8 * KB) for p in range(2)]
    v_sb = [At(f"v_sb{p}", [128, 4096], BF16, 113 * KB + p * 8 * KB) for p in range(2)]
    wqh_sb = At("wqh_sb", [128, NFC * 512], BF16, 129 * KB)
    xm1_sb = At("xm1_sb", [128, 32 * 1024], BF16, 97 * KB)
    colacc_sb = [At(f"colacc{i}", [128, 512], F32, 129 * KB + i * 2 * KB) for i in range(4)]
    attnu_sb = [At(f"attnu{i}", [128, 512], BF16, 137 * KB + i * KB) for i in range(4)]
    wqh1b1 = At("wqh1b1", [128, 4096], BF16, 85 * KB)
    wqh1b2 = At("wqh1b2", [128, 4096], BF16, 179 * KB)
    attnq2_sb = At("attnq2", [128, 8 * 1024], BF16, 165 * KB)
    xb1_sb = At("xb1_sb", [128, NFC * 256], BF16, 161 * KB)
    wbt0_sb = At("wbt0_sb", [128, 4096], BF16, 161 * KB)
    wbt1_sb = At("wbt1_sb", [96, 4096], BF16, 169 * KB)
    pairsum_sb2 = [At(f"pairsum{i}", [128, 512], BF16, 141 * KB + i * KB) for i in range(4)]
    sq_sb = At("sq_sb", [128, 1024], BF16, 177 * KB)
    logits_fm = At("logits_fm", [28, 1024], F32, 179 * KB)
    logits_tm = At("logits_tm", [128, 224], F32, 183 * KB)
    gates_sb = At("gates_sb", [28, 1024], BF16, 184 * KB)
    gtmp = At("gtmp", [128, 64], F32, 186 * KB)
    gsc = At("gsc", [128, 8], F32, 186 * KB + 256)
    geq = At("geq", [128, 64], F32, 186 * KB + 512)
    gate_tm = At("gate_tm", [128, 28], F32, 186 * KB + 768)
    mean_sb = At("mean_sb", [1, 1024], F32, 179 * KB)
    msq_sb = At("msq_sb", [1, 1024], F32, 183 * KB)

    S0 = ctx.enter_context(nc.psum_tensor("S0", [128, 1024], F32))
    S1 = ctx.enter_context(nc.psum_tensor("S1", [128, 1024], F32))
    T0 = ctx.enter_context(nc.psum_tensor("T0", [128, 512], F32))
    T1 = ctx.enter_context(nc.psum_tensor("T1", [128, 512], F32))
    T2 = ctx.enter_context(nc.psum_tensor("T2", [128, 512], F32))
    T3 = ctx.enter_context(nc.psum_tensor("T3", [128, 512], F32))

    SEMS = {}
    for s in ["sldc", "swkv", "sxa0", "sxa1", "swq", "sxb0", "sxb1",
              "sws0", "sws1", "smisc", "sxm", "sdbg",
              "pe", "ve", "se", "po", "ag", "rb", "out0", "out1",
              "cc0", "cc1", "cc2", "cc3"]:
        SEMS[s] = ctx.enter_context(nc.semaphore(s))

    class _DictD(dict):
        def __init__(self, base):
            super().__init__(base)

        def __getitem__(self, k):
            if k is None:
                return 0
            return super().get(k, 0)

        def get(self, k, d=0):
            if k is None:
                return d
            return super().get(k, d)

    au_prev = {}
    queues = {k: [] for k in ("tensor", "scalar", "vector", "sync", "gpsimd")}
    cnt = {s: 0 for s in SEMS}

    def emit(eng, fn):
        queues[eng].append(fn)

    def wait(eng, sem, val):
        if val > 0:
            emit(eng, lambda e, s=sem, v=val: e.wait_ge(SEMS[s], v))

    def eminc(eng, fn, sem, n=1):
        def wrapped(e, _fn=fn, _s=sem, _n=n):
            _fn(e).then_inc(SEMS[_s], _n)
        emit(eng, wrapped)
        cnt[sem] += n
        return cnt[sem]

    def sld(out_ap, in_ap):
        return eminc("sync", lambda e, o=out_ap, i=in_ap: e.dma_start(out=o, in_=i), "sldc", 16)

    def gld(sem, out_ap, in_ap):
        return eminc("gpsimd", lambda e, o=out_ap, i=in_ap: e.dma_start(out=o, in_=i), sem, 16)

    def mm(bank_ap, lhsT_ap, rhs_ap, start, stop, sem=None):
        fn = (lambda e, o=bank_ap, l=lhsT_ap, r=rhs_ap, s=start, st=stop:
              e.matmul(o, l, r, start=s, stop=st, skip_group_check=True))
        if sem is None:
            emit("tensor", lambda e, _f=fn: _f(e))
            return None
        return eminc("tensor", fn, sem)

    sld(identf_sb[:, :], identf)
    sld(ones_sb[:, 0:1], onescol)
    sld(onesrow_sb[:, :], onesrow)
    sld(mask_sb[:, :], maskdiag)
    sld(emat_sb[:, :], emat)
    sld(idx_sb[:, :], idx_rb)
    SLD_CONST = cnt["sldc"]

    emit("vector", lambda e: e.memset(cpool[:, 0:1], SCALE_ATT))
    emit("vector", lambda e: e.memset(cpool[:, 1:2], -0.5))
    emit("vector", lambda e: e.memset(cpool[:, 2:3], EPS))
    emit("vector", lambda e: e.memset(cpool[:, 4:5], 1.0))
    ve_cpool = eminc("vector", lambda e: e.memset(cpool[:, 5:6], -1.0), "ve")

    gld_wkv = gld("swkv", wkv_sb[:, :], wkv_t)
    xa_slots = [xa_sb[0], xa_sb[1], xb0_sb]
    XA_SEM = ["sxa0", "sxa1", "sxb0"]
    pe_tileA, ve_evacA, gld_xa = {}, {}, {}
    for tt in range(3):
        gld_xa[tt] = gld(XA_SEM[tt % 3], xa_slots[tt][:, :],
                         x_t[tt].rearrange("p c t -> p (c t)"))

    for tt in range(TTA):
        xt = xa_slots[tt % 3]
        kb0, kb1, vb = (T0, T1, S0) if tt % 2 == 0 else (T2, T3, S1)
        wait("tensor", "swkv", gld_wkv)
        wait("tensor", XA_SEM[tt % 3], gld_xa[tt])
        if tt >= 2:
            wait("tensor", "ve", ve_evacA[tt - 2])
        for p in range(2):
            kbank = kb0 if p == 0 else kb1
            for fc in range(NFC):
                mm(kbank[:, 0:256],
                   wkv_sb[:, fc * 512 + p * 128: fc * 512 + (p + 1) * 128],
                   xt[:, fc * 256:(fc + 1) * 256], fc == 0, fc == NFC - 1)
        for kt in range(2):
            off = 512 * kt
            for fc in range(NFC):
                mm(vb[:, off: off + 256],
                   xt[:, fc * 256 + kt * 128: fc * 256 + kt * 128 + 128],
                   wkv_sb[:, fc * 512 + 256: fc * 512 + 512], fc == 0, fc == NFC - 1,
                   sem="pe" if (kt == 1 and fc == NFC - 1) else None)
        pe_tileA[tt] = cnt["pe"]

        if tt + 3 < TTA:
            wait("gpsimd", "pe", pe_tileA[tt])
            gld_xa[tt + 3] = gld(XA_SEM[tt % 3], xa_slots[tt % 3][:, :],
                                 x_t[tt + 3].rearrange("p c t -> p (c t)"))
        if tt == 6:
            gld_wqh0 = gld("swq", wqh_sb[:, :], wq_t[0])

        wait("vector", "pe", pe_tileA[tt])
        for p in range(2):
            kbank = kb0 if p == 0 else kb1
            emit("vector", lambda e, o=k_sb[p][:, tt * 256:(tt + 1) * 256], i=kbank[:, 0:256]:
                 e.tensor_copy(o, i))
        for kt in range(2):
            off = 512 * kt
            ktg = tt * 2 + kt
            for p in range(2):
                fn = (lambda e, o=v_sb[p][:, ktg * 128:(ktg + 1) * 128],
                      i=vb[:, off + p * 128: off + (p + 1) * 128]: e.tensor_copy(o, i))
                if kt == 1 and p == 1:
                    eminc("vector", fn, "ve")
                else:
                    emit("vector", fn)
        ve_evacA[tt] = cnt["ve"]

    VE_A_DONE = ve_evacA[TTA - 1]
    if DEBUG:
        wait("sync", "ve", VE_A_DONE)
        for p in range(2):
            eminc("sync", lambda e, o=dbg_k[p], i=k_sb[p][:, :]:
                  e.dma_start(out=o, in_=i), "sdbg", 16)
            eminc("sync", lambda e, o=dbg_v[p], i=v_sb[p][:, :]:
                  e.dma_start(out=o, in_=i), "sdbg", 16)

    pe_tileB, ve_evacB = {}, {}
    QB_BANKS_EVEN = [(T0, 0), (T1, 0), (T2, 0), (T3, 0)]
    QB_BANKS_ODD = [(S0, 0), (S0, 512), (S1, 0), (S1, 512)]
    xb_sb = [xb0_sb, xb1_sb]
    gld_wqh, gld_xb = {}, {}

    for p in range(2):
        if p == 0:
            gld_wqh[0] = gld_wqh0
        else:
            wait("gpsimd", "pe", pe_tileB[(0, TTA - 1)])
            wait("gpsimd", "swq", gld_wqh1b2)
            gld_wqh1c = gld("swq", wqh_sb[:, 0:8192], wq_t[1][:, 8192:16384])
            gld_wqh[p] = max(gld_wqh1b1, gld_wqh1b2, gld_wqh1c)
        for tt in range(2):
            if p == 0 and tt == 0:
                wait("gpsimd", "pe", pe_tileA[14])
            gld_xb[(p, tt)] = gld(f"sxb{tt % 2}", xb_sb[tt][:, :],
                                  x_t[tt].rearrange("p c t -> p (c t)"))
        for tt in range(TTA):
            xt = xb_sb[tt % 2]
            banks = QB_BANKS_EVEN if tt % 2 == 0 else QB_BANKS_ODD
            wait("tensor", "swq", gld_wqh[p] if p == 0 else max(gld_wqh1b1, gld_wqh1b2))
            wait("tensor", f"sxb{tt % 2}", gld_xb[(p, tt)])
            if p == 0 and tt < 2:
                wait("tensor", "ve", VE_A_DONE)
            if (p, tt) == (0, 0):
                pass
            prev = (p, tt - 2) if tt >= 2 else ((p - 1, TTA + tt - 2) if p == 1 else None)
            if prev is not None and prev in ve_evacB:
                wait("tensor", "ve", ve_evacB[prev])
            for fc in range(NFC):
                if p == 1 and tt == 0 and fc == 16:
                    wait("tensor", "swq", gld_wqh1c)
                for g in range(4):
                    bank, off = banks[g]
                    if p == 0:
                        lhs = wqh_sb[:, fc * 512 + g * 128: fc * 512 + (g + 1) * 128]
                    elif fc < 8:
                        lhs = wqh1b1[:, fc * 512 + g * 128: fc * 512 + (g + 1) * 128]
                    elif fc < 16:
                        lhs = wqh1b2[:, (fc - 8) * 512 + g * 128: (fc - 8) * 512 + (g + 1) * 128]
                    else:
                        lhs = wqh_sb[:, (fc - 16) * 512 + g * 128: (fc - 16) * 512 + (g + 1) * 128]
                    mm(bank[:, off: off + 256], lhs,
                       xt[:, fc * 256:(fc + 1) * 256], fc == 0, fc == NFC - 1,
                       sem="pe" if (g == 3 and fc == NFC - 1) else None)
            pe_tileB[(p, tt)] = cnt["pe"]

            if tt + 2 < TTA:
                wait("gpsimd", "pe", pe_tileB[(p, tt)])
                gld_xb[(p, tt + 2)] = gld(f"sxb{tt % 2}", xb_sb[tt % 2][:, :],
                                          x_t[tt + 2].rearrange("p c t -> p (c t)"))
            if p == 0 and tt == 10:
                gld_wqh1b1 = gld("swq", wqh1b1[:, :], wq_t[1][:, 0:4096])
            if p == 0 and tt == 12:
                wait("gpsimd", "swq", gld_wqh1b1)
                gld_wqh1b2 = gld("swq", wqh1b2[:, :], wq_t[1][:, 4096:8192])

            wait("vector", "pe", pe_tileB[(p, tt)])
            for g in range(4):
                bank, off = banks[g]
                fn = (lambda e, o=q_sb[p][:, g * 4096 + tt * 256: g * 4096 + (tt + 1) * 256],
                      i=bank[:, off: off + 256]: e.tensor_copy(o, i))
                if g == 3:
                    eminc("vector", fn, "ve")
                else:
                    emit("vector", fn)
            ve_evacB[(p, tt)] = cnt["ve"]

    VE_B_DONE = ve_evacB[(1, TTA - 1)]
    if DEBUG:
        wait("sync", "ve", VE_B_DONE)
        for p in range(2):
            eminc("sync", lambda e, o=dbg_q[p], i=q_sb[p][:, 0:4096]:
                  e.dma_start(out=o, in_=i), "sdbg", 16)

    attnq_bufs = [attnq_sb, attnq2_sb]
    QORDER = [3, 2, 1, 0]

    sbank_reader_se = {0: 0, 1: 0}
    expbuf_pe = {}
    expbuf_ve = {}
    expbuf_po = {}
    pairsum_po = {0: 0, 1: 0, 2: 0, 3: 0}
    pairsum_po_also = {}
    pend_ps = {0: None, 1: None}
    group_pool_started = {0: False, 1: False}
    last_pair_of_group = [False]
    pe_sums, pe_bcast, se_bcast, se_rec = {}, {}, {}, {}
    ve_rec, ve_evacA_attn, ve_norm2 = {}, {}, {}
    po_group = {}
    EXPB = {0: exp_sb, 1: exp2_sb}
    SBANK = {0: S0, 1: S1}
    ABANK = {0: T0, 1: T1}

    def gkey(qoi, s, ml):
        return qoi * 16 + s * 8 + ml

    wait("tensor", "ve", max(VE_B_DONE, ve_cpool))
    wait("tensor", "sldc", SLD_CONST)
    wait("vector", "sldc", SLD_CONST)
    wait("vector", "ve", ve_cpool)
    wait("scalar", "ve", ve_cpool)

    def emit_scores(s, g, j, kts, m, rhs_q):
        wait("tensor", "se", sbank_reader_se[s])
        for (jj, kk) in enumerate(kts):
            mm(SBANK[s][:, jj * 512:(jj + 1) * 512],
               k_sb[s][:, kk * 128:(kk + 1) * 128], rhs_q, True, True,
               sem="pe" if jj == len(kts) - 1 else None)
        return cnt["pe"]

    def emit_exp_chain(s, g, j, kts, m, pe_scores, slot):
        wait("scalar", "pe", max(pe_scores, expbuf_pe.get((s, slot), 0)))
        wait("scalar", "ve", expbuf_ve.get((s, slot), 0))
        wait("scalar", "po", expbuf_po.get((s, slot), 0))
        ncols = 512 * len(kts)
        se_val = eminc("scalar",
                       lambda e, o=EXPB[s][slot][:, 0:ncols],
                       i=SBANK[s][:, 0:ncols], sc=cpool[:, 0:1]:
                       e.activation(o, i, AF.Exp, scale=sc), "se")
        sbank_reader_se[s] = se_val
        has_diag = kts[-1] == m
        wait("vector", "se", se_val)
        if has_diag:
            doff = 512 * (len(kts) - 1)
            vm = eminc("vector", lambda e, o=EXPB[s][slot][:, doff:doff + 512],
                       mk=mask_sb[:, :]: e.tensor_tensor(o, o, mk, ALU.mult), "ve")
            wait("vector", "ve", vm)
            att_wait = ("ve", vm)
        else:
            att_wait = ("se", se_val)
        ca = s * 2 + (g % 2)
        first_pool = (j <= 1)
        if len(kts) == 2:
            ps_slot = s * 2 + (j % 2)
            wait("vector", "po", pairsum_po[ps_slot])
            ve_ps = eminc("vector",
                          lambda e, o=pairsum_sb2[ps_slot][:, :], a=EXPB[s][slot][:, 0:512],
                          b=EXPB[s][slot][:, 512:1024]: e.tensor_tensor(o, a, b, ALU.add), "ve")
            expbuf_ve[(s, slot)] = ve_ps
            if j % 2 == 0 and not last_pair_of_group[0]:
                pend_ps[s] = (ps_slot, ve_ps)
                return att_wait
            if j % 2 == 1:
                pslot0, ve0 = pend_ps[s]
                pend_ps[s] = None
                wait("vector", "ve", max(ve0, ve_ps))
                ve_ps = eminc("vector",
                              lambda e, o=pairsum_sb2[ps_slot][:, :],
                              a=pairsum_sb2[pslot0][:, :]:
                              e.tensor_tensor(o, o, a, ALU.add), "ve")
                wait("vector", "ve", ve_ps)
                pairsum_po_also[ps_slot] = pslot0
            pool_src = pairsum_sb2[ps_slot][:, :]
            wait("gpsimd", "ve", ve_ps)
        else:
            pool_src = EXPB[s][slot][:, 0:512]
            wait("gpsimd", att_wait[0], att_wait[1])
            if pend_ps[s] is not None:
                pslot0, ve0 = pend_ps[s]
                pend_ps[s] = None
                if first_pool:
                    wait("gpsimd", "pe", pe_sums.get(gkey_prev2(s, g), 0))
                    first_pool = False
                wait("gpsimd", "ve", ve0)
                if group_pool_started[s]:
                    fnp = (lambda e, o=colacc_sb[ca][:, :], i=pairsum_sb2[pslot0][:, :]:
                           e.tensor_tensor(o, o, i, ALU.add))
                else:
                    fnp = (lambda e, o=colacc_sb[ca][:, :], i=pairsum_sb2[pslot0][:, :]:
                           e.tensor_copy(o, i))
                    group_pool_started[s] = True
                po_p = eminc("gpsimd", fnp, "po")
                wait("gpsimd", "po", po_p)
                pairsum_po[pslot0] = po_p
        if first_pool:
            wait("gpsimd", "pe", pe_sums.get(gkey_prev2(s, g), 0))
        if group_pool_started[s]:
            fn = (lambda e, o=colacc_sb[ca][:, :], i=pool_src: e.tensor_tensor(o, o, i, ALU.add))
        else:
            fn = (lambda e, o=colacc_sb[ca][:, :], i=pool_src: e.tensor_copy(o, i))
            group_pool_started[s] = True
        po_v = eminc("gpsimd", fn, "po")
        wait("gpsimd", "po", po_v)
        if len(kts) == 2:
            ps_slot = s * 2 + (j % 2)
            pairsum_po[ps_slot] = po_v
            other = pairsum_po_also.pop(ps_slot, None)
            if other is not None:
                pairsum_po[other] = po_v
        else:
            expbuf_po[(s, slot)] = po_v
        return att_wait

    def emit_attn2(s, g, j, kts, m, slot, att_wait):
        wait("tensor", att_wait[0], att_wait[1])
        if kts[0] == 0:
            wait("tensor", "ve", ve_evacA_attn.get(gkey_prev1(s, g), 0))
        for (jj, kk) in enumerate(kts):
            mm(ABANK[s][:, 0:512], v_sb[s][:, kk * 128:(kk + 1) * 128],
               EXPB[s][slot][:, jj * 512:(jj + 1) * 512], kk == 0, kk == m,
               sem="pe" if jj == len(kts) - 1 else None)
        expbuf_pe[(s, slot)] = cnt["pe"]
        return cnt["pe"]

    seq_of_stream = {0: [], 1: []}

    def gkey_prev1(s, g):
        lst = seq_of_stream[s]
        return lst[-1] if lst else None

    def gkey_prev2(s, g):
        lst = seq_of_stream[s]
        return lst[-2] if len(lst) >= 2 else None

    _ps, _van2, _vea = pe_sums, ve_norm2, ve_evacA_attn
    pe_sums = _DictD(_ps)
    ve_norm2 = _DictD(_van2)
    ve_evacA_attn = _DictD(_vea)

    def pairs_of(m):
        kts, out, kt = m + 1, [], 0
        while kt < kts:
            ks = [kt] if kt + 1 >= kts else [kt, kt + 1]
            out.append(ks)
            kt += len(ks)
        return out

    tail_sums = []
    tail_norm = []

    last_sums = [None]
    rec_slot = {}
    rec_ctr = [0]

    def emit_tail_sums():
        while tail_sums:
            gk, s, po_done = tail_sums.pop(0)
            wait("tensor", "po", po_done)
            prev_gk = last_sums[0]
            if prev_gk is not None:
                wait("tensor", "se", ve_rec.get(prev_gk, 0))
            last_sums[0] = gk
            ca = s * 2 + (sums_meta[gk] % 2)
            pe_sums[gk] = mm(T2[0:1, 0:512], cpool[:, 4:5], colacc_sb[ca][:, :],
                             True, True, sem="pe")
            rs = rec_ctr[0] % 2
            rec_slot[gk] = rs
            rec_ctr[0] += 1
            wait("scalar", "pe", pe_sums[gk])
            prevprev = rec_user.get(rs)
            if prevprev is not None:
                wait("scalar", "pe", pe_bcast.get(prevprev, 0))
            rec_user[rs] = gk
            if prev_gk is not None:
                wait("scalar", "se", se_rec[prev_gk])
            se_l = eminc("scalar",
                         lambda e: e.activation(lntmp_sb[:, :], T2[0:1, 0:512], AF.Ln), "se")
            wait("scalar", "se", se_l)
            se_rec[gk] = eminc("scalar",
                               lambda e, o=recip_pool[rs][:, :]:
                               e.activation(o, lntmp_sb[:, :], AF.Exp,
                                            scale=cpool[0:1, 5:6]), "se")
            ve_rec[gk] = se_l

    last_bcast = [None]
    bc_slot = {}
    bc_ctr = [0]

    def emit_tail_norm():
        while tail_norm:
            gk, s, qoi_, ml_ = tail_norm.pop(0)
            aqbuf = attnq_bufs[qoi_ % 2]
            wait("tensor", "se", se_rec[gk])
            if last_bcast[0] is not None:
                wait("tensor", "se", se_bcast.get(last_bcast[0], 0))
            last_bcast[0] = gk
            bs = bc_ctr[0] % 2
            bc_slot[gk] = bs
            bc_ctr[0] += 1
            pe_bcast[gk] = mm(T3[:, 0:512], onesrow_sb[:, :], recip_pool[rec_slot[gk]][:, :],
                              True, True, sem="pe")
            wait("scalar", "pe", pe_bcast[gk])
            prevprev = bc_user.get(bs)
            if prevprev is not None:
                wait("scalar", "ve", ve_norm2.get(prevprev, 0))
            bc_user[bs] = gk
            se_bcast[gk] = eminc("scalar",
                                 lambda e, o=bcast_sb[bs][:, :], i=T3[:, 0:512]:
                                 e.copy(o, i), "se")
            wait("vector", "se", se_bcast[gk])
            if ml_ == 0 and qoi_ >= 2 and s == 0:
                wait("vector", "ag", 16 * (qoi_ - 1))
            ua = au_of[gk]
            for g_ in range(4):
                ch = s * 4 + g_
                fn = (lambda e,
                      o=aqbuf[:, ch * 1024 + ml_ * 128: ch * 1024 + (ml_ + 1) * 128],
                      i0=attnu_sb[ua][:, g_ * 128:(g_ + 1) * 128],
                      i1=bcast_sb[bs][:, g_ * 128:(g_ + 1) * 128]:
                      e.tensor_tensor(o, i0, i1, ALU.mult))
                if g_ == 3:
                    ve_norm2[gk] = eminc("vector", fn, "ve")
                else:
                    emit("vector", fn)

    def quarter_store(qoi, Q):
        wait("sync", "ve", ve_norm2[store_gate[qoi]])
        wait("sync", "ag", 16 * qoi)
        eminc("sync", lambda e, o=agin[Q].rearrange("(c x) t -> x c t", x=128),
              i=attnq_bufs[qoi % 2].rearrange("x (c t) -> x c t", c=8):
              e.dma_start(out=o, in_=i), "ag", 16)
        wait("gpsimd", "ag", 16 * (qoi + 1))
        eminc("gpsimd", lambda e, i=agin[Q], o=agout[Q * 4096:(Q + 1) * 4096, :]:
              e.collective_compute("AllGather", ALU.bypass,
                                   replica_groups=[[0, 1, 2, 3], [4, 5, 6, 7]],
                                   ins=[i.opt()], outs=[o.opt()]), f"cc{Q}")

    sums_order, bcast_order = [], []
    rec_user, bc_user = {}, {}
    sums_meta, au_of = {}, {}
    store_gate = {}
    pending_store = None
    au_rot = 0

    for qoi, Q in enumerate(QORDER):
        for ml in range(8):
            m = 8 * Q + ml
            pairs = pairs_of(m)
            J = len(pairs)
            rhs = {s: q_sb[s].rearrange("x (g t) -> x g t", g=4)[:, :, m * 128:(m + 1) * 128]
                   for s in range(2)}
            gks = {s: gkey(qoi, s, ml) for s in range(2)}
            for s in range(2):
                sums_meta[gks[s]] = len(seq_of_stream[s])
            att_waits = {}
            pe_sc = {}
            group_pool_started[0] = False
            group_pool_started[1] = False
            for j, kts in enumerate(pairs):
                slot = j % 2
                last_pair_of_group[0] = (j == J - 1)
                for s in range(2):
                    pe_sc[s] = emit_scores(s, gks[s], j, kts, m, rhs[s])
                if j == 0:
                    emit_tail_sums()
                if j == 1 or (J == 1 and j == 0):
                    emit_tail_norm()
                    if pending_store is not None:
                        quarter_store(*pending_store)
                        pending_store = None
                for s in range(2):
                    att_waits[(s, slot)] = emit_exp_chain(s, gks[s], j, kts, m, pe_sc[s], slot)
                if j > 0:
                    pslot = (j - 1) % 2
                    for s in range(2):
                        emit_attn2(s, gks[s], j - 1, pairs[j - 1], m, pslot,
                                   att_waits[(s, pslot)])
            pslot = (J - 1) % 2
            pe_att_end = {}
            for s in range(2):
                pe_att_end[s] = emit_attn2(s, gks[s], J - 1, pairs[J - 1], m, pslot,
                                           att_waits[(s, pslot)])
            for s in range(2):
                gk = gks[s]
                ua = au_rot % 4
                au_rot += 1
                au_of[gk] = ua
                wait("vector", "pe", pe_att_end[s])
                wait("vector", "ve", ve_norm2.get(au_prev.get(ua), 0) if True else 0)
                fn = (lambda e, o=attnu_sb[ua][:, :], i=ABANK[s][:, 0:512]:
                      e.tensor_copy(o, i))
                ve_evacA_attn[gk] = eminc("vector", fn, "ve")
                au_prev[ua] = gk
                tail_sums.append((gk, s, cnt["po"]))
                tail_norm.append((gk, s, qoi, ml))
                sums_order.append(gk)
                bcast_order.append(gk)
                seq_of_stream[s].append(gk)
            if ml == 7:
                store_gate[qoi] = gks[1]
                pending_store = (qoi, Q)

    emit_tail_sums()
    emit_tail_norm()
    quarter_store(*pending_store)
    pending_store = None

    if DEBUG:
        wait("sync", "ve", ve_norm2[store_gate[3]])
        eminc("sync", lambda e, o=dbg_aq, i=attnq_bufs[1][:, :]: e.dma_start(out=o, in_=i),
              "sdbg", 16)
        eminc("sync", lambda e: e.dma_start(out=dbg_rc, in_=recip_pool[0][:, :]), "sdbg", 16)
        for i_ in range(2):
            eminc("sync", lambda e, o=dbg_ca[i_], i=colacc_sb[i_][:, :]: e.dma_start(out=o, in_=i), "sdbg", 16)
            eminc("sync", lambda e, o=dbg_bc[i_], i=bcast_sb[i_][:, :]: e.dma_start(out=o, in_=i), "sdbg", 16)

    gld_wd = {}
    gld_wd[0] = gld("sws0", wstr_sb[0][:, :], wd_t[0])
    gld_wd[1] = gld("sws1", wstr_sb[1][:, :], wd_t[1])
    for q in range(4):
        wait("gpsimd", f"cc{q}", 1)
    wait("gpsimd", "sldc", SLD_CONST)
    rb_vals = {}
    gld_xm = {}
    for g in range(4):
        if g > 0:
            wait("gpsimd", "rb", 16 * 8 * g)
        if g == 1:
            gld_xm[0] = gld("sxm", xm1_sb[:, 0:16 * 1024],
                            xm_t.rearrange("p c t -> p (c t)")[:, 0:16 * 1024])
        for i_ in range(8):
            ic = g * 8 + i_
            eminc("gpsimd",
                  lambda e, o=attn_mine[:, ic * 1024:(ic + 1) * 1024],
                  i=agout[:, :], ix=idx_sb[:, ic:ic + 1]:
                  e.indirect_dma_start(out=o, out_offset=None, in_=i,
                                       in_offset=bass.IndirectOffsetOnAxis(ix, 0)),
                  "rb", 16)
    for ic in range(32):
        rb_vals[ic] = 16 * 8 * min(4, (ic // 8) + 1)
    wait("gpsimd", "sxm", 16)
    gld_xm[1] = gld("sxm", xm1_sb[:, 16 * 1024:32 * 1024],
                    xm_t.rearrange("p c t -> p (c t)")[:, 16 * 1024:32 * 1024])


    if DEBUG:
        wait("sync", "rb", cnt["rb"])
        eminc("sync", lambda e, o=dbg_am, i=attn_mine[:, :]: e.dma_start(out=o, in_=i),
              "sdbg", 16)

    def stats_mms(oc, src_sb, last_sem=None):
        mm(T0[0:1, 0:512], ones_sb[:, 0:1], src_sb[:, oc * 1024: oc * 1024 + 512],
           oc == 0, oc == 31)
        mm(T1[0:1, 0:512], ones_sb[:, 0:1], src_sb[:, oc * 1024 + 512:(oc + 1) * 1024],
           oc == 0, oc == 31)
        mm(T2[0:1, 0:512], ones_sb[:, 0:1], sq_sb[:, 0:512], oc == 0, oc == 31)
        return mm(T3[0:1, 0:512], ones_sb[:, 0:1], sq_sb[:, 512:1024], oc == 0, oc == 31,
                  sem=last_sem if last_sem else "pe")

    def ln_math_and_bcast(pe_stats_done):
        wait("vector", "pe", pe_stats_done)

        def vstep(fn):
            v = eminc("vector", fn, "ve")
            wait("vector", "ve", v)
            return v
        vstep(lambda e: e.tensor_copy(mean_sb[:, 0:512], T0[0:1, 0:512]))
        vstep(lambda e: e.tensor_copy(mean_sb[:, 512:1024], T1[0:1, 0:512]))
        vstep(lambda e: e.tensor_copy(msq_sb[:, 0:512], T2[0:1, 0:512]))
        vstep(lambda e: e.tensor_copy(msq_sb[:, 512:1024], T3[0:1, 0:512]))
        vstep(lambda e: e.tensor_scalar(mean_sb[:, :], mean_sb[:, :],
                                        1.0 / 4096.0, None, ALU.mult))
        vstep(lambda e: e.tensor_scalar(msq_sb[:, :], msq_sb[:, :],
                                        1.0 / 4096.0, None, ALU.mult))
        vstep(lambda e: e.tensor_tensor(m2_sb[:, :], mean_sb[:, :],
                                        mean_sb[:, :], ALU.mult))
        ve_var = vstep(lambda e: e.tensor_tensor(msq_sb[:, :], msq_sb[:, :],
                                                 m2_sb[:, :], ALU.subtract))
        wait("scalar", "ve", ve_var)
        se_ln = eminc("scalar", lambda e: e.activation(msq_sb[:, :], msq_sb[:, :], AF.Ln,
                                                       bias=cpool[0:1, 2:3]), "se")
        wait("scalar", "se", se_ln)
        se_rstd = eminc("scalar", lambda e: e.activation(msq_sb[:, :], msq_sb[:, :], AF.Exp,
                                                         scale=cpool[0:1, 1:2]), "se")
        wait("tensor", "se", se_rstd)
        mm(S0[:, 0:512], onesrow_sb[:, :], mean_sb[:, 0:512], True, True)
        mm(S0[:, 512:1024], onesrow_sb[:, :], mean_sb[:, 512:1024], True, True)
        mm(S1[:, 0:512], onesrow_sb[:, :], msq_sb[:, 0:512], True, True)
        pe_bc = mm(S1[:, 512:1024], onesrow_sb[:, :], msq_sb[:, 512:1024], True, True,
                   sem="pe")
        wait("scalar", "pe", pe_bc)
        emit("scalar", lambda e: e.copy(mean_bc[:, :], S0[:, :]))
        se_cp = eminc("scalar", lambda e: e.copy(rstd_bc[:, :], S1[:, :]), "se")
        return se_cp

    pe_oc, ve_add, se_sq, pe_stats = {}, {}, {}, {}
    for oc in range(32):
        bank = S0 if oc % 2 == 0 else S1
        wait("tensor", f"sws{oc % 2}", gld_wd[oc])
        if oc == 0:
            for q in range(4):
                wait("tensor", f"cc{q}", 1)
        if oc >= 2:
            wait("tensor", "ve", ve_add[oc - 2])
        for half in range(2):
            for ic in range(NFC):
                if oc == 0 and half == 0:
                    wait("tensor", "rb", rb_vals[ic])
                mm(bank[:, half * 512:(half + 1) * 512],
                   wstr_sb[oc % 2][:, ic * 128:(ic + 1) * 128],
                   attn_mine[:, ic * 1024 + half * 512: ic * 1024 + half * 512 + 512],
                   ic == 0, ic == NFC - 1,
                   sem="pe" if (half == 1 and ic == NFC - 1) else None)
        pe_oc[oc] = cnt["pe"]
        if oc >= 1:
            po = oc - 1
            wait("tensor", "ve", ve_add[po])
            wait("tensor", "se", se_sq[po])
            pe_stats[po] = stats_mms(po, xm1_sb)
        if oc + 2 < 32:
            wait("gpsimd", "pe", pe_oc[oc])
            gld_wd[oc + 2] = gld(f"sws{oc % 2}", wstr_sb[oc % 2][:, :], wd_t[oc + 2])

        wait("vector", "pe", pe_oc[oc])
        if oc == 0:
            wait("vector", "sxm", 16)
        if oc == 16:
            wait("vector", "sxm", 32)
        for half in range(2):
            fn = (lambda e, o=xm1_sb[:, oc * 1024 + half * 512: oc * 1024 + half * 512 + 512],
                  i0=bank[:, half * 512:(half + 1) * 512]:
                  e.tensor_tensor(o, i0, o, ALU.add))
            if half == 1:
                ve_add[oc] = eminc("vector", fn, "ve")
            else:
                emit("vector", fn)
        wait("scalar", "ve", ve_add[oc])
        if oc >= 1:
            wait("scalar", "pe", pe_stats[oc - 1])
        se_sq[oc] = eminc("scalar",
                          lambda e, o=sq_sb[:, :], i=xm1_sb[:, oc * 1024:(oc + 1) * 1024]:
                          e.activation(o, i, AF.Square), "se")

    wait("tensor", "ve", ve_add[31])
    wait("tensor", "se", se_sq[31])
    pe_stats[31] = stats_mms(31, xm1_sb)
    se_ln1 = ln_math_and_bcast(pe_stats[31])

    ve_n1 = {}
    for oc in range(32):
        if oc == 0:
            wait("vector", "se", se_ln1)
        vv = eminc("vector", lambda e, o=xm1_sb[:, oc * 1024:(oc + 1) * 1024], mb=mean_bc[:, :]:
                   e.tensor_tensor(o, o, mb, ALU.subtract), "ve")
        wait("vector", "ve", vv)
        ve_n1[oc] = eminc("vector",
                          lambda e, o=xm1_sb[:, oc * 1024:(oc + 1) * 1024], rb=rstd_bc[:, :]:
                          e.tensor_tensor(o, o, rb, ALU.mult), "ve")

    if DEBUG:
        wait("sync", "ve", ve_n1[31])
        eminc("sync", lambda e, o=dbg_x1, i=xm1_sb[:, :]: e.dma_start(out=o, in_=i),
              "sdbg", 16)

    gld_wg = gld("smisc", wg_sb[:, :], wg_t)
    gld_wa = {}
    wait("gpsimd", "pe", pe_oc[31])
    gld_wa[0] = gld("sws0", wstr_sb[0][:, 0:16 * ER], wa_t[0])
    gld_wa[1] = gld("sws1", wstr_sb[1][:, 0:16 * ER], wa_t[1])
    gld_wbt0 = gld("smisc", wbt0_sb[:, :], wbt0_t)
    gld_wbt1 = gld("smisc", wbt1_sb[:, :], wbt1_t)
    SMISC_TOTAL = cnt["smisc"]

    wait("tensor", "smisc", SMISC_TOTAL)
    for half in range(2):
        gb = T0 if half == 0 else T1
        for fc in range(NFC):
            if half == 0:
                wait("tensor", "ve", ve_n1[fc])
            mm(gb[0:28, 0:512], wg_sb[:, fc * 28:(fc + 1) * 28],
               xm1_sb[:, fc * 1024 + half * 512: fc * 1024 + half * 512 + 512],
               fc == 0, fc == NFC - 1,
               sem="pe" if (half == 1 and fc == NFC - 1) else None)
    pe_gate = cnt["pe"]
    wait("vector", "pe", pe_gate)
    emit("vector", lambda e: e.tensor_copy(logits_fm[:, 0:512], T0[0:28, 0:512]))
    ve_logits = eminc("vector",
                      lambda e: e.tensor_copy(logits_fm[:, 512:1024], T1[0:28, 0:512]), "ve")

    wait("tensor", "ve", ve_logits)
    for t in range(8):
        fn = (lambda e, o=T2[:, t * 28:(t + 1) * 28], i=logits_fm[0:28, t * 128:(t + 1) * 128],
              iden=identf_sb[0:28, 0:28]: e.transpose(o, i, iden))
        if t == 7:
            pe_tr1 = eminc("tensor", fn, "pe")
        else:
            emit("tensor", fn)
    wait("vector", "pe", pe_tr1)
    ve_ltm = eminc("vector", lambda e: e.tensor_copy(logits_tm[:, :], T2[:, 0:224]), "ve")

    h_sched = []
    for half in range(2):
        for fc in range(NFC):
            wb_ = wstr_sb[fc // 16]
            fo = (fc % 16) * ER
            h_sched.append((S0[:, half * 512:(half + 1) * 512], wb_[:, fo: fo + 128],
                            xm1_sb[:, fc * 1024 + half * 512: fc * 1024 + half * 512 + 512],
                            fc == 0, fc == NFC - 1))
    for half in range(2):
        for fc in range(NFC):
            wb_ = wstr_sb[fc // 16]
            fo = (fc % 16) * ER
            h_sched.append((S1[0:96, half * 512:(half + 1) * 512], wb_[:, fo + 128: fo + 224],
                            xm1_sb[:, fc * 1024 + half * 512: fc * 1024 + half * 512 + 512],
                            fc == 0, fc == NFC - 1))
    h_pos = [0]

    def emit_h(n):
        for (o, l, r, s, st) in h_sched[h_pos[0]: h_pos[0] + n]:
            mm(o, l, r, s, st)
        h_pos[0] += n

    pe_tr2 = {}
    for t in range(8):
        lt = logits_tm[:, t * 28:(t + 1) * 28]
        d1, masked = gtmp[:, 0:28], gtmp[:, 32:60]
        eq1, eq2 = geq[:, 0:28], geq[:, 32:60]
        m1v, m2v, ee, ee2, uu2, uu, p2v = (gsc[:, 0:1], gsc[:, 1:2], gsc[:, 2:3],
                                           gsc[:, 3:4], gsc[:, 4:5], gsc[:, 5:6],
                                           gsc[:, 6:7])

        def vstep(fn):
            v = eminc("vector", fn, "ve")
            wait("vector", "ve", v)
            return v
        if t == 0:
            wait("vector", "ve", ve_ltm)
        else:
            wait("vector", "pe", pe_tr2[t - 1])
        vstep(lambda e, lt=lt, m1=m1v: e.reduce_max(m1, lt, axis=AX.X))
        vstep(lambda e, lt=lt, m1=m1v, d=d1: e.tensor_scalar(d, lt, m1, None, ALU.subtract))
        vstep(lambda e, d=d1, q=eq1: e.tensor_single_scalar(q, d, 0.0, ALU.is_ge))
        vstep(lambda e, q=eq1, mk=masked: e.tensor_scalar(mk, q, 1e30, None, ALU.mult))
        vstep(lambda e, lt=lt, mk=masked: e.tensor_tensor(mk, lt, mk, ALU.subtract))
        vstep(lambda e, mk=masked, m2=m2v: e.reduce_max(m2, mk, axis=AX.X))
        vstep(lambda e, mk=masked, m2=m2v, d=d1: e.tensor_scalar(d, mk, m2, None, ALU.subtract))
        vstep(lambda e, d=d1, q=eq2: e.tensor_single_scalar(q, d, 0.0, ALU.is_ge))
        ve_ee = vstep(lambda e, m1=m1v, m2=m2v, o=ee: e.tensor_tensor(o, m2, m1, ALU.subtract))
        wait("scalar", "ve", ve_ee)
        se_ee = eminc("scalar", lambda e, o=ee2, i=ee: e.activation(o, i, AF.Exp), "se")
        wait("vector", "se", se_ee)
        vstep(lambda e, o=uu2, i=ee2: e.tensor_scalar(o, i, 1.0, None, ALU.add))
        vstep(lambda e, o=uu, i=uu2: e.reciprocal(o, i))
        vstep(lambda e, o=p2v, a=ee2, b=uu: e.tensor_tensor(o, a, b, ALU.mult))
        vstep(lambda e, q=eq1, s=uu: e.tensor_scalar(q, q, s, None, ALU.mult))
        vstep(lambda e, q=eq2, s=p2v: e.tensor_scalar(q, q, s, None, ALU.mult))
        ve_gate_done = vstep(lambda e, a=eq1, b=eq2, o=gate_tm[:, :]:
                             e.tensor_tensor(o, a, b, ALU.add))
        bank = T3 if t < 4 else T2
        boff = (t % 4) * 128
        wait("tensor", "ve", ve_gate_done)
        if t == 4:
            wait("tensor", "ve", ve_ltm)
        pe_tr2[t] = eminc("tensor",
                          lambda e, o=bank[0:28, boff:boff + 128], i=gate_tm[:, :],
                          iden=identf_sb[:, :]: e.transpose(o, i, iden), "pe")
        if t == 0:
            wait("tensor", "sws0", gld_wa[0])
            wait("tensor", "sws1", gld_wa[1])
        emit_h(32)
        if t == 3:
            wait("vector", "pe", pe_tr2[3])
            emit("vector", lambda e: e.tensor_copy(gates_sb[0:28, 0:512], T3[0:28, 0:512]))
        if t == 7:
            wait("vector", "pe", pe_tr2[7])
            ve_gates = eminc("vector",
                             lambda e: e.tensor_copy(gates_sb[0:28, 512:1024], T2[0:28, 0:512]),
                             "ve")

    emit_h(len(h_sched) - h_pos[0])
    wait("tensor", "ve", ve_gates)
    for half in range(2):
        mm((T0 if half == 0 else T1)[:, 0:512], emat_sb[:, 0:128],
           gates_sb[0:28, half * 512:(half + 1) * 512], True, True)
        pe_hgx = mm((T2 if half == 0 else T3)[0:96, 0:512], emat_sb[:, 128:224],
                    gates_sb[0:28, half * 512:(half + 1) * 512], True, True, sem="pe")
    wait("scalar", "pe", pe_hgx)
    emit("scalar", lambda e: e.copy(ge0_sb[:, 0:512], T0[:, 0:512]))
    emit("scalar", lambda e: e.copy(ge0_sb[:, 512:1024], T1[:, 0:512]))
    emit("scalar", lambda e: e.copy(ge1_sb[0:96, 0:512], T2[0:96, 0:512]))
    se_ge = eminc("scalar", lambda e: e.copy(ge1_sb[0:96, 512:1024], T3[0:96, 0:512]), "se")
    wait("vector", "se", se_ge)
    emit("vector", lambda e: e.tensor_tensor(hg0_sb[:, 0:512], S0[:, 0:512], ge0_sb[:, 0:512], ALU.mult))
    emit("vector", lambda e: e.tensor_tensor(hg0_sb[:, 512:1024], S0[:, 512:1024], ge0_sb[:, 512:1024], ALU.mult))
    emit("vector", lambda e: e.tensor_tensor(hg1_sb[0:96, 0:512], S1[0:96, 0:512], ge1_sb[0:96, 0:512], ALU.mult))
    ve_hg = eminc("vector", lambda e: e.tensor_tensor(hg1_sb[0:96, 512:1024], S1[0:96, 512:1024],
                                                      ge1_sb[0:96, 512:1024], ALU.mult), "ve")

    gld_wb = {}
    wait("gpsimd", "pe", pe_hgx)
    gld_wb[0] = gld("sws0", wstr_sb[0][:, :], wb_t[0])
    gld_wb[1] = gld("sws1", wstr_sb[1][:, :], wb_t[1])
    pe_oc2, ve_add2, se_sq2, pe_stats2 = {}, {}, {}, {}
    for oc in range(32):
        bank = S0 if oc % 2 == 0 else S1
        wait("tensor", f"sws{oc % 2}", gld_wb[oc])
        if oc == 0:
            wait("tensor", "ve", ve_hg)
            wait("tensor", "se", se_ge)
        if oc >= 2:
            wait("tensor", "ve", ve_add2[oc - 2])
        for half in range(2):
            for fc in range(NFC):
                mm(bank[:, half * 512:(half + 1) * 512],
                   wstr_sb[oc % 2][:, fc * 128:(fc + 1) * 128],
                   xm1_sb[:, fc * 1024 + half * 512: fc * 1024 + half * 512 + 512],
                   fc == 0, False)
            mm(bank[:, half * 512:(half + 1) * 512], wbt0_sb[:, oc * 128:(oc + 1) * 128],
               hg0_sb[:, half * 512:(half + 1) * 512], False, False)
            mm(bank[:, half * 512:(half + 1) * 512], wbt1_sb[0:96, oc * 128:(oc + 1) * 128],
               hg1_sb[0:96, half * 512:(half + 1) * 512], False, True,
               sem="pe" if half == 1 else None)
        pe_oc2[oc] = cnt["pe"]
        if oc >= 1:
            po = oc - 1
            wait("tensor", "ve", ve_add2[po])
            wait("tensor", "se", se_sq2[po])
            pe_stats2[po] = stats_mms(po, attn_mine)
        if oc + 2 < 32:
            wait("gpsimd", "pe", pe_oc2[oc])
            gld_wb[oc + 2] = gld(f"sws{oc % 2}", wstr_sb[oc % 2][:, :], wb_t[oc + 2])

        wait("vector", "pe", pe_oc2[oc])
        for half in range(2):
            fn = (lambda e, o=attn_mine[:, oc * 1024 + half * 512: oc * 1024 + half * 512 + 512],
                  i0=bank[:, half * 512:(half + 1) * 512],
                  i1=xm1_sb[:, oc * 1024 + half * 512: oc * 1024 + half * 512 + 512]:
                  e.tensor_tensor(o, i0, i1, ALU.add))
            if half == 1:
                ve_add2[oc] = eminc("vector", fn, "ve")
            else:
                emit("vector", fn)
        wait("scalar", "ve", ve_add2[oc])
        if oc >= 1:
            wait("scalar", "pe", pe_stats2[oc - 1])
        se_sq2[oc] = eminc("scalar",
                           lambda e, o=sq_sb[:, :], i=attn_mine[:, oc * 1024:(oc + 1) * 1024]:
                           e.activation(o, i, AF.Square), "se")

    if DEBUG:
        wait("sync", "ve", ve_hg)
        eminc("sync", lambda e, o=dbg_hg, i=hg0_sb[:, :]: e.dma_start(out=o, in_=i), "sdbg", 16)
        wait("sync", "ve", ve_add2[31])
        eminc("sync", lambda e, o=dbg_x2, i=attn_mine[:, :]: e.dma_start(out=o, in_=i), "sdbg", 16)
    wait("tensor", "ve", ve_add2[31])
    wait("tensor", "se", se_sq2[31])
    pe_stats2[31] = stats_mms(31, attn_mine)
    se_ln2 = ln_math_and_bcast(pe_stats2[31])

    for oc in range(32):
        if oc == 0:
            wait("vector", "se", se_ln2)
        if oc >= 2:
            wait("vector", f"out{oc % 2}", 16 * (oc // 2))
        st = out_stage[oc % 2]
        vv = eminc("vector", lambda e, o=st[:, :], i=attn_mine[:, oc * 1024:(oc + 1) * 1024],
                   mb=mean_bc[:, :]: e.tensor_tensor(o, i, mb, ALU.subtract), "ve")
        wait("vector", "ve", vv)
        ve_o = eminc("vector", lambda e, o=st[:, :], rb=rstd_bc[:, :]:
                     e.tensor_tensor(o, o, rb, ALU.mult), "ve")
        wait("sync", "ve", ve_o)
        eminc("sync", lambda e, o=outT[oc * 128:(oc + 1) * 128, :], i=st[:, :]:
              e.dma_start(out=o, in_=i), f"out{oc % 2}", 16)
    wait("sync", "out0", cnt["out0"])
    wait("sync", "out1", cnt["out1"])
    if DEBUG:
        wait("sync", "sdbg", cnt["sdbg"])

    with nc.Block() as block:
        @block.tensor
        def _(e):
            for fn in queues["tensor"]:
                fn(e)

        @block.scalar
        def _(e):
            for fn in queues["scalar"]:
                fn(e)

        @block.vector
        def _(e):
            for fn in queues["vector"]:
                fn(e)

        @block.sync
        def _(e):
            for fn in queues["sync"]:
                fn(e)

        @block.gpsimd
        def _(e):
            for fn in queues["gpsimd"]:
                fn(e)

    ctx.__exit__(None, None, None)
    return nc


_NC_CACHE = [None]


def _prep_inputs(L0_attn_q, L0_attn_w_qkv, L0_attn_w_dense, L0_ffn_base_W,
                 L0_ffn_gate_W, L0_ffn_lora_A, L0_ffn_lora_B):
    bf = ml_dtypes.bfloat16
    x = np.ascontiguousarray(L0_attn_q, dtype=np.float32)
    wqkv = np.ascontiguousarray(L0_attn_w_qkv, dtype=np.float32)
    wdense = np.ascontiguousarray(L0_attn_w_dense, dtype=np.float32)
    baseW = np.ascontiguousarray(L0_ffn_base_W, dtype=np.float32)
    gateW = np.ascontiguousarray(L0_ffn_gate_W, dtype=np.float32)
    loraA = np.ascontiguousarray(L0_ffn_lora_A, dtype=np.float32)
    loraB = np.ascontiguousarray(L0_ffn_lora_B, dtype=np.float32)

    wd_t = np.ascontiguousarray(
        wdense.T.reshape(32, 128, 32, 128).transpose(2, 1, 0, 3).reshape(32, 128, 32 * 128))
    wb_t = np.ascontiguousarray(
        baseW.T.reshape(32, 128, 32, 128).transpose(2, 1, 0, 3).reshape(32, 128, 32 * 128))
    wg_t = np.ascontiguousarray(gateW.T.reshape(32, 128, 28).transpose(1, 0, 2).reshape(128, 32 * 28))
    a_flat = loraA.reshape(224, 4096)
    wa_full = np.ascontiguousarray(a_flat.T.reshape(32, 128, 224).transpose(1, 0, 2))
    wa_t = np.ascontiguousarray(
        wa_full.reshape(128, 2, 16, 224).transpose(1, 0, 2, 3).reshape(2, 128, 16 * 224))
    bt_flat = np.ascontiguousarray(loraB.transpose(0, 2, 1).reshape(224, 4096))
    wbt0_t = np.ascontiguousarray(bt_flat[0:128])
    wbt1_t = np.ascontiguousarray(bt_flat[128:224])

    identf = np.eye(128, dtype=np.float32)
    onescol = np.ones((128, 1), dtype=bf)
    onesrow = np.ones((1, 128), dtype=np.float32)
    kq = np.arange(128)
    mask = (kq[:, None] <= kq[None, :]).astype(bf)
    maskdiag = np.ascontiguousarray(np.tile(mask, (1, 4)))
    emat = np.zeros((28, 224), dtype=np.float32)
    for e_ in range(28):
        emat[e_, e_ * 8:(e_ + 1) * 8] = 2.0
    emat = emat.astype(bf)

    in_maps = []
    for c in range(8):
        b, r = c // 4, c % 4
        xT = np.ascontiguousarray(x[b].T)
        x_t = np.ascontiguousarray(
            xT.reshape(32, 128, 16, 256).transpose(2, 1, 0, 3).reshape(TTA, 128, 32, 256))
        xm = np.ascontiguousarray(
            xT[:, r * 1024:(r + 1) * 1024].reshape(32, 128, 1024).transpose(1, 0, 2))
        wq_h = []
        for p in range(2):
            kv = 2 * r + p
            rows = wqkv[(kv * 4) * 128:(kv * 4 + 4) * 128]
            wq_h.append(rows.T.reshape(32, 128, 4, 128).transpose(1, 0, 2, 3)
                        .reshape(128, 32 * 512))
        wq_t = np.ascontiguousarray(np.stack(wq_h))
        kvrows = []
        for p in range(2):
            kv = 2 * r + p
            kvrows.append(wqkv[4096 + kv * 128: 4096 + (kv + 1) * 128])
        for p in range(2):
            kv = 2 * r + p
            kvrows.append(wqkv[5120 + kv * 128: 5120 + (kv + 1) * 128])
        wkv = np.concatenate(kvrows, axis=0)
        wkv_t = np.ascontiguousarray(
            wkv.T.reshape(32, 128, 4, 128).transpose(1, 0, 2, 3).reshape(128, 32 * 512))
        ic = np.arange(32)
        rows = r * 4096 + (ic // 8) * 1024 + (ic % 8) * 128
        idx = (rows[None, :] + np.arange(128)[:, None]).astype(np.int32)
        in_maps.append({
            "x_t": x_t, "xm_t": xm, "wq_t": wq_t, "wkv_t": wkv_t,
            "wd_t": wd_t, "wb_t": wb_t, "wg_t": wg_t, "wa_t": wa_t,
            "wbt0_t": wbt0_t, "wbt1_t": wbt1_t,
            "identf": identf, "onescol": onescol, "onesrow": onesrow,
            "maskdiag": maskdiag, "emat": emat, "idx_rb": idx,
        })
    return in_maps


def kernel(L0_attn_q, L0_attn_w_qkv, L0_attn_w_dense, L0_ffn_base_W,
           L0_ffn_gate_W, L0_ffn_lora_A, L0_ffn_lora_B):
    in_maps = _prep_inputs(L0_attn_q, L0_attn_w_qkv, L0_attn_w_dense,
                           L0_ffn_base_W, L0_ffn_gate_W, L0_ffn_lora_A, L0_ffn_lora_B)
    if _NC_CACHE[0] is None:
        _NC_CACHE[0] = build_nc()
    nc = _NC_CACHE[0]
    res = run_bass_kernel_spmd(nc, in_maps, core_ids=list(range(8)))
    out = np.empty((2, 4096, 4096), dtype=np.float32)
    for c in range(8):
        b, r = c // 4, c % 4
        out[b, r * 1024:(r + 1) * 1024, :] = np.asarray(res.results[c]["outT"]).T
    return out
